# version 11
# baseline (speedup 1.0000x reference)
import numpy as np
import ml_dtypes
import concourse.bass as bass
import concourse.mybir as mybir
from concourse.bass_utils import run_bass_kernel_spmd

F32 = mybir.dt.float32
BF16 = mybir.dt.bfloat16
AF = mybir.ActivationFunctionType
ALU = mybir.AluOpType
AX = mybir.AxisListType

NCORES = 8
S = 4096
D = 2048
H = 16
NB = 4
II = 8192
TS = 512
NT = S // TS
TBN = TS // 128
NC_D = D // 128
NC_I = II // 128
EPS_LN = 1e-5
EPS_ATT = 1e-6
DEBUG = False

STREAM = {"pe": "pe", "dve": "dve", "act": "act", "actd": "act",
          "gp": "gp", "gpd": "gp", "sync": "sync"}
ENGSTREAMS = ("pe", "dve", "act", "gp", "sync")


class Buf:
    __slots__ = ("name", "w", "r", "frozen")

    def __init__(self, name):
        self.name = name
        self.w = None
        self.r = []
        self.frozen = False


class Prog:
    def __init__(self):
        self.ops = {e: [] for e in ENGSTREAMS}
        self.cnt = {k: 0 for k in STREAM}
        self.maxwait = {e: {} for e in ENGSTREAMS}

    def group(self, key, fns, reads=(), writes=(), dma=False, extra=()):
        stream = STREAM[key]
        waits = {}

        def addtok(t):
            if t is not None:
                waits[t[0]] = max(waits.get(t[0], 0), t[1])
        for b in reads:
            addtok(b.w)
        for b in writes:
            addtok(b.w)
            for t in b.r:
                addtok(t)
        for t in extra:
            addtok(t)
        mw = self.maxwait[stream]
        wl = []
        for sk in sorted(waits):
            v = waits[sk]
            if mw.get(sk, 0) >= v:
                continue
            wl.append((sk, v))
            mw[sk] = v
        inc = 16 * len(fns) if dma else 1
        self.cnt[key] += inc
        tok = (key, self.cnt[key])
        self.ops[stream].append((wl, fns, key, dma))
        for b in writes:
            b.w = tok
            b.r = []
        for b in reads:
            if not b.frozen:
                b.r.append(tok)
        return tok


def hadamard(n):
    h = np.array([[1.0]])
    while h.shape[0] < n:
        h = np.block([[h, h], [h, -h]])
    return h


def build_program():
    nc = bass.Bass("TRN2", target_bir_lowering=False)

    x_d = nc.dram_tensor("x", [S, D], F32, kind="ExternalInput").ap()
    wq_d = nc.dram_tensor("wq", [D, D], F32, kind="ExternalInput").ap()
    wo_d = nc.dram_tensor("wo", [D, D], F32, kind="ExternalInput").ap()
    bw_d = nc.dram_tensor("bw", [H, NB], F32, kind="ExternalInput").ap()
    mem_d = nc.dram_tensor("mem", [NB, H, 128, 128], F32, kind="ExternalInput").ap()
    mn_d = nc.dram_tensor("mn", [NB, H, 128], F32, kind="ExternalInput").ap()
    suug_d = nc.dram_tensor("suug", [D], F32, kind="ExternalInput").ap()
    sudp_d = nc.dram_tensor("sudp", [II], F32, kind="ExternalInput").ap()
    wsug_d = nc.dram_tensor("wsug", [2 * II], F32, kind="ExternalInput").ap()
    wsdp_d = nc.dram_tensor("wsdp", [D], F32, kind="ExternalInput").ap()
    wup_d = nc.dram_tensor("wup", [II, D], F32, kind="ExternalInput").ap()
    wgate_d = nc.dram_tensor("wgate", [II, D], F32, kind="ExternalInput").ap()
    wdown_d = nc.dram_tensor("wdown", [D, II], F32, kind="ExternalInput").ap()
    h128_d = nc.dram_tensor("h128", [128, 128], BF16, kind="ExternalInput").ap()
    idbf_d = nc.dram_tensor("idbf", [128, 128], BF16, kind="ExternalInput").ap()
    idf_d = nc.dram_tensor("idf", [128, 128], F32, kind="ExternalInput").ap()
    nsel_d = nc.dram_tensor("nsel", [64, NB], F32, kind="ExternalInput").ap()
    out_d = nc.dram_tensor("out", [S, D], F32, kind="ExternalOutput").ap()

    wq_nat = nc.dram_tensor("wq_nat", [D, D], BF16).ap()
    wo_nat = nc.dram_tensor("wo_nat", [D, D], BF16).ap()
    wup_nat = nc.dram_tensor("wup_nat", [II, D], BF16).ap()
    wgate_nat = nc.dram_tensor("wgate_nat", [II, D], BF16).ap()
    wd_nat = nc.dram_tensor("wd_nat", [D, II], BF16).ap()
    img_q = nc.dram_tensor("img_q", [8, 128, 4096], BF16).ap()
    img_wo = nc.dram_tensor("img_wo", [8, 128, 4096], BF16).ap()
    img_ug = nc.dram_tensor("img_ug", [NC_I, 128, 4096], BF16).ap()
    scr_g = nc.dram_tensor("scr_g", [4, NC_I, 128, 512], BF16).ap()
    wdT_nat = nc.dram_tensor("wdT_nat", [II, D], BF16).ap()
    smallscr = nc.dram_tensor("smallscr", [64], F32).ap()

    RESERVE = 212000
    _resv = nc.alloc_sbuf_tensor("kv_arena_reserve", [128, RESERVE], mybir.dt.uint8)
    SBUF_BASE = nc.lookup_mloc(_resv).addr

    def at(name, shape, dtype, off):
        return nc.alloc_sbuf_tensor_at(name, shape, dtype, offset=SBUF_BASE + off).ap()

    OFF_ARENA = 0
    arena_bf = at("arena_bf", [128, 32768], BF16, OFF_ARENA)
    y0_f = at("y0_f", [128, 8192], F32, OFF_ARENA)
    OFF_ACC = 65536
    acc_f = at("acc_f", [128, 8192], F32, OFF_ACC)
    xnf_bf = at("xnf_bf", [128, 8192], BF16, OFF_ACC)
    OFF_X2 = 98304
    x2_f = at("x2_f", [128, 4096], F32, OFF_X2)
    y2_bf = at("y2_bf", [128, 8192], BF16, 114688)
    xld_f = at("xld_f", [128, 4096], F32, 114688)
    xn_tm = at("xn_tm", [128, 2048], BF16, 131072)
    imgqw = [at(f"imgqw{i}", [128, 4096], BF16, 135168 + i * 8192) for i in range(2)]
    imgug = [at(f"imgug{i}", [128, 4096], BF16, 151552 + i * 8192) for i in range(2)]
    imggd = [at(f"imggd{i}", [128, 2048], BF16, 167936 + i * 4096) for i in range(2)]
    wsbc_f = at("wsbc_f", [128, 2048], F32, 167936)
    mem_sb = at("mem_sb", [128, 8192], BF16, 176128)

    base = [192512]

    def nx(name, shape, dtype):
        t = at(name, shape, dtype, base[0])
        sz = shape[1] * (2 if dtype == BF16 else 4)
        base[0] += (sz + 31) // 32 * 32
        return t
    mn_sb = nx("mn_sb", [128, 64], BF16)
    h128_sb = nx("h128_sb", [128, 128], BF16)
    idbf_sb = nx("idbf_sb", [128, 128], BF16)
    idf_sb = nx("idf_sb", [128, 128], F32)
    ones1_f = nx("ones1_f", [1, 128], F32)
    onebf = nx("onebf", [128, 1], BF16)
    wact4 = nx("wact4", [128, 256], F32)
    su_fm = nx("su_fm", [128, 16], F32)
    sclu_fm = nx("sclu_fm", [128, 64], F32)
    sclg_fm = nx("sclg_fm", [128, 64], F32)
    sudp_fm = nx("sudp_fm", [128, 64], F32)
    off_sigA = base[0]
    sigA = nx("sigA", [128, 512], F32)
    sigA_bf = at("sigA_bf", [128, 1024], BF16, off_sigA)
    sigB = nx("sigB", [128, 512], F32)
    ctA = nx("ctA", [128, 128], F32)
    ctB = nx("ctB", [128, 128], F32)
    ct_bf = nx("ct_bf", [128, 128], BF16)
    off_ysp = base[0]
    yspare = nx("yspare", [128, 512], F32)
    yspare_bf = at("yspare_bf", [128, 1024], BF16, off_ysp)
    stat = nx("stat", [128, 32], F32)
    bwt = nx("bwt", [16, 32], F32)
    nsel_sb = nx("nsel_sb", [64, NB], F32)
    mnrow = nx("mnrow", [64, 128], BF16)
    vrow = nx("vrow", [64, 128], F32)
    wsrow = nx("wsrow", [1, 512], F32)
    d64 = nx("d64", [64, 4], F32)
    epsln = nx("epsln", [128, 1], F32)
    usb = nx("usb", [128, 512], BF16)
    gsb = nx("gsb", [128, 512], BF16)
    stg = [nx(f"stg{i}", [128, 512], BF16) for i in range(2)]
    assert base[0] <= RESERVE, f"sbuf overflow {base[0]}"

    big = [nc.alloc_psum_tensor(f"big{i}", [128, 512], F32).ap() for i in range(5)]
    ptr = [nc.alloc_psum_tensor(f"ptr{i}", [128, 1024], BF16).ap() for i in range(2)]
    misc = nc.alloc_psum_tensor("misc", [128, 512], F32).ap()
    assert nc.psum_banks_remaining == 0, nc.psum_banks_remaining

    p = Prog()

    AR = [Buf(f"ar{i}") for i in range(128)]

    def arc(byte0, nbytes):
        return AR[byte0 // 512:(byte0 + nbytes + 511) // 512]
    ACC = [Buf(f"acc{i}") for i in range(64)]

    def accc(byte0, nbytes):
        return ACC[byte0 // 512:(byte0 + nbytes + 511) // 512]
    X2C = [Buf(f"x2c{i}") for i in range(32)]

    def x2c(byte0, nbytes):
        return X2C[byte0 // 512:(byte0 + nbytes + 511) // 512]
    B_big = [Buf(f"pbig{i}") for i in range(5)]
    B_ptr = [Buf(f"pptr{i}") for i in range(2)]
    B_misc = Buf("pmisc")
    B_xn_tm = Buf("xn_tm")
    B_xnf = [Buf(f"xnf{c}") for c in range(NC_D)]
    B_y2 = [Buf(f"y2_{c}") for c in range(NC_D)]
    B_imgqw = [Buf("imgqw0"), Buf("imgqw1")]
    B_imgug = [Buf("imgug0"), Buf("imgug1")]
    B_imgug_g = [Buf("imgug0g"), Buf("imgug1g")]
    B_imgqw_g = Buf("imgqw0g")
    B_imggd = [Buf("imggd0"), Buf("imggd1")]
    B_stat = [Buf(f"stat{tb}") for tb in range(TBN)]
    B_rt = Buf("rt")
    B_sigA = Buf("sigA")
    B_sigB = Buf("sigB")
    B_ctA = Buf("ctA")
    B_ctB = Buf("ctB")
    B_ctbf = Buf("ct_bf")
    B_ysp = Buf("yspare")
    B_usb = Buf("usb")
    B_gsb = Buf("gsb")
    B_stg = [Buf("stg0"), Buf("stg1")]
    B_misc_sb = Buf("misc_sb")
    B_mem = Buf("mem_sb")
    B_consts = Buf("consts")
    B_scr_q = [Buf(f"scrq{i}") for i in range(8)]
    B_scr_wo = [Buf(f"scrwo{i}") for i in range(8)]
    B_scr_ug = [Buf(f"scrug{i}") for i in range(NC_I)]
    B_scr_g = [Buf(f"scrg{i}") for i in range(4)]
    B_scr_sm = Buf("scr_small")
    B_scr_wdT = Buf("scr_wdT")
    B_outc = [[Buf(f"out{t}_{k}") for k in range(17)] for t in range(NT)]

    def xslot(t, tb):
        if tb >= 2:
            k = tb - 2
            return x2_f[:, k * 2048:(k + 1) * 2048], x2c(k * 8192, 8192)
        k = tb
        return xld_f[:, k * 2048:(k + 1) * 2048], list(B_y2[8 * k:8 * k + 8])
    for tb in range(TBN):
        sl, bufs = xslot(0, tb)
        p.group("sync", [lambda e, tb=tb, sl=sl: e.dma_start(
            sl, x_d[tb * 128:(tb + 1) * 128, :])], writes=bufs, dma=True)
    p.group("sync", [lambda e: e.dma_start(h128_sb, h128_d),
                     lambda e: e.dma_start(idbf_sb, idbf_d),
                     lambda e: e.dma_start(idf_sb, idf_d),
                     lambda e: e.dma_start(nsel_sb, nsel_d)],
            writes=[B_consts], dma=True)
    p.group("dve", [lambda e: e.memset(ones1_f, 1.0)], writes=[B_consts])
    p.group("dve", [lambda e: e.memset(onebf, 1.0)], writes=[B_consts])
    p.group("dve", [lambda e: e.memset(epsln, EPS_LN)], writes=[B_consts])

    cast_toks = {}

    def cast(name, dst, src, nchunks):
        rows = src.shape[0]
        cr = rows // nchunks
        cast_toks[name] = [
            p.group("gpd", [lambda e, k=k: e.dma_start(dst[k * cr:(k + 1) * cr, :],
                                                       src[k * cr:(k + 1) * cr, :])],
                    dma=True) for k in range(nchunks)]
        cast_toks[name + "_cr"] = cr
    def cast_tok(name, row0, nrows):
        cr = cast_toks[name + "_cr"]
        assert row0 // cr == (row0 + nrows - 1) // cr
        return cast_toks[name][row0 // cr]

    memfns = []
    for hh in range(H):
        for n in range(NB):
            ti = hh * NB + n
            memfns.append(lambda e, n=n, hh=hh, ti=ti: e.dma_start(
                mem_sb[:, ti * 128:(ti + 1) * 128], mem_d[n, hh]))
    p.group("gpd", memfns, writes=[B_mem], dma=True)
    p.group("gpd", [lambda e: e.dma_start(mnrow, mn_d.rearrange("n h d -> (n h) d"))],
            writes=[B_misc_sb], dma=True)
    p.group("pe", [lambda e: e.transpose(ptr[0][:, 0:64], mnrow, idbf_sb[0:64, 0:64])],
            reads=[B_misc_sb, B_consts], writes=[B_ptr[0]])
    p.group("dve", [lambda e: e.tensor_copy(mn_sb, ptr[0][:, 0:64])],
            reads=[B_ptr[0]], writes=[B_mem])

    def vec_fm(dst, src_ap, rows):
        p.group("gpd", [lambda e: e.dma_start(vrow[0:rows, :], src_ap)],
                writes=[B_misc_sb], dma=True)
        p.group("pe", [lambda e: e.transpose(misc[:, 64:64 + rows], vrow[0:rows, :],
                                             idf_sb[0:rows, 0:rows])],
                reads=[B_misc_sb, B_consts], writes=[B_misc])
        p.group("dve", [lambda e: e.tensor_copy(dst, misc[:, 64:64 + rows])],
                reads=[B_misc], writes=[B_consts])
    vec_fm(su_fm, suug_d.rearrange("(a b) -> a b", b=128), 16)
    vec_fm(sclu_fm, wsug_d[0:II].rearrange("(a b) -> a b", b=128), 64)
    vec_fm(sclg_fm, wsug_d[II:2 * II].rearrange("(a b) -> a b", b=128), 64)
    vec_fm(sudp_fm, sudp_d.rearrange("(a b) -> a b", b=128), 64)
    p.group("dve", [lambda e: e.tensor_mul(sclu_fm, sclu_fm, sudp_fm)],
            reads=[B_consts], writes=[B_consts])
    p.group("dve", [lambda e: e.tensor_scalar_mul(sclu_fm, sclu_fm, D ** -0.5)],
            reads=[B_consts], writes=[B_consts])
    p.group("dve", [lambda e: e.tensor_scalar_mul(sclg_fm, sclg_fm, D ** -0.5)],
            reads=[B_consts], writes=[B_consts])

    for os in range(4):
        p.group("gpd", [lambda e, os=os: e.dma_start(
            wsrow, wsdp_d.rearrange("(a b) -> a b", b=512)[os:os + 1, :])],
            writes=[B_misc_sb], dma=True)
        p.group("pe", [lambda e: e.matmul(big[4], ones1_f, wsrow, start=True, stop=True)],
                reads=[B_misc_sb, B_consts], writes=[B_big[4]])
        p.group("dve", [lambda e, os=os: e.tensor_scalar(
            wsbc_f[:, os * 512:(os + 1) * 512], big[4], II ** -0.5, None, ALU.mult)],
            reads=[B_big[4]], writes=[B_imggd[0], B_imggd[1]])

    p.group("gpd", [lambda e: e.dma_start(bwt[0:16, 0:4], bw_d)], writes=[B_misc_sb], dma=True)
    p.group("dve", [lambda e: e.reduce_max(bwt[0:16, 4:5], bwt[0:16, 0:4], axis=AX.X)],
            reads=[B_misc_sb], writes=[B_misc_sb])
    p.group("dve", [lambda e: e.tensor_scalar(bwt[0:16, 8:12], bwt[0:16, 0:4],
                                              bwt[0:16, 4:5], None, ALU.subtract)],
            reads=[B_misc_sb], writes=[B_misc_sb])
    p.group("act", [lambda e: e.activation(bwt[0:16, 12:16], bwt[0:16, 8:12], AF.Exp)],
            reads=[B_misc_sb], writes=[B_misc_sb])
    p.group("dve", [lambda e: e.reduce_sum(bwt[0:16, 16:17], bwt[0:16, 12:16], axis=AX.X)],
            reads=[B_misc_sb], writes=[B_misc_sb])
    p.group("dve", [lambda e: e.reciprocal(bwt[0:16, 20:21], bwt[0:16, 16:17])],
            reads=[B_misc_sb], writes=[B_misc_sb])
    p.group("dve", [lambda e: e.tensor_scalar(bwt[0:16, 24:28], bwt[0:16, 12:16],
                                              bwt[0:16, 20:21], None, ALU.mult)],
            reads=[B_misc_sb], writes=[B_misc_sb])
    p.group("pe", [lambda e: e.matmul(misc[0:64, 128:129], mn_sb, onebf, start=True, stop=True)],
            reads=[B_mem, B_consts], writes=[B_misc])
    p.group("dve", [lambda e: e.tensor_copy(d64[:, 0:1], misc[0:64, 128:129])],
            reads=[B_misc], writes=[B_misc_sb])
    p.group("pe", [lambda e: e.matmul(misc[0:1, 160:164], d64[:, 0:1], nsel_sb,
                                      start=True, stop=True)],
            reads=[B_misc_sb, B_consts], writes=[B_misc])
    p.group("dve", [lambda e: e.tensor_single_scalar(d64[0:1, 0:4], misc[0:1, 160:164],
                                                     EPS_ATT, ALU.is_ge)],
            reads=[B_misc], writes=[B_misc_sb])
    p.group("pe", [lambda e: e.matmul(misc[0:16, 192:196], ones1_f[0:1, 0:16],
                                      d64[0:1, 0:4], start=True, stop=True)],
            reads=[B_misc_sb, B_consts], writes=[B_misc])
    p.group("dve", [lambda e: e.tensor_tensor(bwt[0:16, 28:32], bwt[0:16, 24:28],
                                              misc[0:16, 192:196], ALU.mult)],
            reads=[B_misc, B_misc_sb], writes=[B_misc_sb])
    p.group("gpd", [lambda e: e.dma_start(smallscr.rearrange("(a b) -> a b", b=4),
                                          bwt[0:16, 28:32])],
            reads=[B_misc_sb], writes=[B_scr_sm], dma=True)
    for tb in range(TBN):
        p.group("gpd", [lambda e, tb=tb: e.dma_start(
            wsrow[:, 0:256].rearrange("o (h r) -> o h r", r=16)[:, :, tb * 4:(tb + 1) * 4],
            smallscr.rearrange("(o h n) -> o h n", o=1, n=4))],
            reads=[B_scr_sm], writes=[B_misc_sb], dma=True)
    p.group("pe", [lambda e: e.matmul(misc[:, 224:480], ones1_f, wsrow[0:1, 0:256],
                                      start=True, stop=True)],
            reads=[B_misc_sb, B_consts], writes=[B_misc])
    p.group("dve", [lambda e: e.tensor_copy(wact4, misc[:, 224:480])],
            reads=[B_misc], writes=[B_consts])

    crd = D // 4
    rows4 = II // 4
    cast_toks["wd"] = []
    cast_toks["wd_cr"] = crd
    cast_toks["wup"] = []
    cast_toks["wgate"] = []
    cast_toks["wup_cr"] = rows4
    cast_toks["wgate_cr"] = rows4

    def wd_chunk(k):
        cast_toks["wd"].append(p.group("gpd", [
            lambda e, k=k: e.dma_start(wd_nat[k * crd:(k + 1) * crd, :],
                                       wdown_d[k * crd:(k + 1) * crd, :])], dma=True))

    def ug_chunk(k):
        cast_toks["wup"].append(p.group("gpd", [
            lambda e, k=k: e.dma_start(wup_nat[k * rows4:(k + 1) * rows4, :],
                                       wup_d[k * rows4:(k + 1) * rows4, :])], dma=True))
        cast_toks["wgate"].append(p.group("gpd", [
            lambda e, k=k: e.dma_start(wgate_nat[k * rows4:(k + 1) * rows4, :],
                                       wgate_d[k * rows4:(k + 1) * rows4, :])], dma=True))

    cast("wq", wq_nat, wq_d, 1)
    def emit_wq_prep():
        for fc in range(NC_D):
            slot = fc % 2
            p.group("actd", [lambda e, fc=fc, slot=slot: e.dma_start(
                imgqw[slot][:, 0:2048],
                wq_nat[:, fc * 128:(fc + 1) * 128], transpose=True)],
                writes=[B_imgqw[slot]], dma=True, extra=[cast_tok("wq", 0, D)])
            p.group("gpd", [lambda e, fc=fc, slot=slot: e.dma_start(
                img_q[:, :, fc * 256:(fc + 1) * 256].rearrange("g p c -> p g c"),
                imgqw[slot][:, 0:2048].rearrange("p (g c) -> p g c", g=8))],
                reads=[B_imgqw[slot]], writes=[B_scr_q[dcg] for dcg in range(8)],
                dma=True)

    cast("wo", wo_nat, wo_d, 1)
    def emit_wo_prep():
        for cf in range(NC_D):
            slot = cf % 2
            hf = cf // 8
            k = cf % 8
            p.group("actd", [lambda e, cf=cf, slot=slot: e.dma_start(
                imgqw[slot][:, 0:2048],
                wo_nat[:, cf * 128:(cf + 1) * 128], transpose=True)],
                writes=[B_imgqw[slot]], dma=True, extra=[cast_tok("wo", 0, D)])
            p.group("gpd", [lambda e, hf=hf, k=k, slot=slot: e.dma_start(
                img_wo[hf::2, :, k * 512:(k + 1) * 512].rearrange("g p c -> p g c"),
                imgqw[slot][:, 0:2048].rearrange("p (g c) -> p g c", g=4))],
                reads=[B_imgqw[slot]],
                writes=[B_scr_wo[2 * os + hf] for os in range(4)], dma=True)

    def emit_ug_prep(blk):
            for w_i, (nat, coloff) in enumerate(((wup_nat, 0), (wgate_nat, 2048))):
                nm = "wup" if w_i == 0 else "wgate"
                for fc in range(NC_D):
                    slot = (blk * NC_D * 2 + w_i * NC_D + fc) % 2
                    p.group("actd", [lambda e, blk=blk, fc=fc, slot=slot, nat=nat: e.dma_start(
                        imgqw[slot][:, 0:2048],
                        nat[blk * 2048:(blk + 1) * 2048, fc * 128:(fc + 1) * 128],
                        transpose=True)],
                        writes=[B_imgqw[slot]], dma=True,
                        extra=[cast_tok(nm, blk * 2048, 2048)])
                    p.group("gpd", [lambda e, blk=blk, fc=fc, slot=slot, coloff=coloff: e.dma_start(
                        img_ug[16 * blk:16 * blk + 16, :, coloff + fc * 128:coloff + (fc + 1) * 128]
                        .rearrange("g p c -> p g c"),
                        imgqw[slot][:, 0:2048].rearrange("p (g c) -> p g c", g=16))],
                        reads=[B_imgqw[slot]],
                        writes=[B_scr_ug[16 * blk + g] for g in range(16)], dma=True)

    def emit_wdT_prep(half):
        for ib in range(32 * half, 32 * (half + 1)):
            slot = ib % 2
            p.group("actd", [lambda e, ib=ib, slot=slot: e.dma_start(
                imgqw[slot][:, 0:2048],
                wd_nat[:, ib * 128:(ib + 1) * 128], transpose=True)],
                writes=[B_imgqw[slot]], dma=True,
                extra=[cast_toks["wd"][3]])
            p.group("gpd", [lambda e, ib=ib, slot=slot: e.dma_start(
                wdT_nat[ib * 128:(ib + 1) * 128, :], imgqw[slot][:, 0:2048])],
                reads=[B_imgqw[slot]], writes=[B_scr_wdT], dma=True)

    emit_wq_prep()
    wd_chunk(0)
    wd_chunk(1)
    wd_chunk(2)
    wd_chunk(3)
    emit_wdT_prep(0)
    emit_wo_prep()
    emit_wdT_prep(1)
    ug_chunk(0)
    ug_chunk(1)
    ug_chunk(2)
    ug_chunk(3)

    def gtile_ap(so, width_bf=256):
        kind, off = so
        if kind == "ar":
            return arena_bf[:, off // 2: off // 2 + width_bf]
        if kind == "sigA":
            return sigA_bf[:, 0:width_bf]
        return yspare_bf[:, 0:width_bf]

    def gtile_bufs(so, width_bf=256):
        kind, off = so
        if kind == "ar":
            return arc(off, width_bf * 2)
        return [B_sigA if kind == "sigA" else B_ysp]

    def emit_g_pass(pp):
        os4 = pp // 2
        half = pp % 2
        gslot = {a: ("ar", 32768 + a * 512) for a in range(64)}
        p.group("sync", [lambda e, pp=pp: e.dma_start(
            arena_bf[:, 16384:32768].rearrange("p (g c) -> p g c", c=256),
            wdT_nat[:, pp * 256:(pp + 1) * 256].rearrange("(g p) c -> p g c", p=128))],
            reads=[B_scr_wdT], writes=arc(32768, 32768), dma=True)
        for ic in range(NC_I):
            so = gslot[ic]
            p.group("pe", [lambda e, so=so: e.matmul(
                big[4][:, 0:256], h128_sb, gtile_ap(so), start=True, stop=True)],
                reads=gtile_bufs(so) + [B_consts], writes=[B_big[4]])
            p.group("dve", [lambda e, so=so, pp=pp: e.tensor_tensor(
                gtile_ap(so), big[4][:, 0:256],
                wsbc_f[:, pp * 256:(pp + 1) * 256], ALU.mult)],
                reads=[B_big[4], B_imggd[0], B_imggd[1]], writes=gtile_bufs(so))
        free = [("sigA", 0), ("ysp", 0)]
        eng_i = 0
        for st in range(6):
            bit = 1 << st
            for a in range(64):
                if a & bit:
                    continue
                b = a | bit
                sa, sb_ = gslot[a], gslot[b]
                f1, f2 = free[0], free[1]
                e1 = ["dve", "dve", "dve", "dve", "dve", "gp"][eng_i % 6]; eng_i += 1
                e2 = ["dve", "dve", "dve", "dve", "dve", "gp"][eng_i % 6]; eng_i += 1
                p.group(e1, [lambda e, sa=sa, sb_=sb_, f1=f1: e.tensor_tensor(
                    gtile_ap(f1), gtile_ap(sa), gtile_ap(sb_), ALU.add)],
                    reads=gtile_bufs(sa) + gtile_bufs(sb_), writes=gtile_bufs(f1))
                p.group(e2, [lambda e, sa=sa, sb_=sb_, f2=f2: e.tensor_tensor(
                    gtile_ap(f2), gtile_ap(sa), gtile_ap(sb_), ALU.subtract)],
                    reads=gtile_bufs(sa) + gtile_bufs(sb_), writes=gtile_bufs(f2))
                gslot[a], gslot[b] = f1, f2
                free = [sa, sb_]
        for a in range(64):
            so = gslot[a]
            p.group("gpd", [lambda e, so=so, os4=os4, half=half, a=a: e.dma_start(
                scr_g[os4, a, :, half * 256:(half + 1) * 256], gtile_ap(so))],
                reads=gtile_bufs(so), writes=[B_scr_g[os4]], dma=True)

    for b in (B_consts, B_mem):
        b.frozen = True
    for b in B_scr_q + B_scr_wo + B_scr_ug + B_scr_g:
        b.frozen = True

    dbg_outs = {}

    def tap(name, ap, bufs):
        if not DEBUG:
            return
        od = nc.dram_tensor(f"dbg_{name}", list(ap.shape), ap.dtype,
                            kind="ExternalOutput").ap()
        dbg_outs[name] = od
        p.group("gpd", [lambda e, od=od, ap=ap: e.dma_start(od, ap)],
                reads=bufs, dma=True)

    for t in range(NT):
        row0 = t * TS
        p.group("gpd", [lambda e, row0=row0: e.dma_start(
            out_d[row0:row0 + TS, :], x_d[row0:row0 + TS, :])],
            writes=[B_outc[t][k] for k in range(17)], dma=True)

        for tb in range(TBN):
            o = 8 * tb
            sl, slbufs = xslot(t, tb)
            p.group("dve", [lambda e, sl=sl, o=o: e.scalar_tensor_tensor(
                xn_tm, sl, 1.0, sl, ALU.bypass, ALU.mult,
                accum_out=stat[:, o + 4:o + 5])],
                reads=slbufs, writes=[B_xn_tm, B_stat[tb]])
            p.group("dve", [lambda e, sl=sl, o=o: e.reduce_sum(
                stat[:, o:o + 1], sl, axis=AX.X)],
                reads=slbufs, writes=[B_stat[tb]])
            p.group("dve", [
                lambda e, o=o: e.tensor_scalar(stat[:, o + 2:o + 3], stat[:, o:o + 1],
                                               1.0 / D, None, ALU.mult),
                lambda e, o=o: e.tensor_scalar(stat[:, o + 3:o + 4], stat[:, o + 4:o + 5],
                                               1.0 / D, None, ALU.mult)],
                reads=[B_stat[tb]], writes=[B_stat[tb]])
            p.group("dve", [lambda e, o=o: e.tensor_tensor(
                stat[:, o + 5:o + 6], stat[:, o + 2:o + 3], stat[:, o + 2:o + 3], ALU.mult)],
                reads=[B_stat[tb]], writes=[B_stat[tb]])
            p.group("dve", [lambda e, o=o: e.tensor_tensor(
                stat[:, o + 6:o + 7], stat[:, o + 3:o + 4], stat[:, o + 5:o + 6],
                ALU.subtract)],
                reads=[B_stat[tb]], writes=[B_stat[tb]])
            p.group("act", [lambda e, o=o: e.activation(
                stat[:, o + 5:o + 6], stat[:, o + 6:o + 7], AF.Ln, bias=epsln)],
                reads=[B_stat[tb], B_consts], writes=[B_stat[tb]])
            p.group("act", [lambda e, o=o: e.activation(
                stat[:, o + 6:o + 7], stat[:, o + 5:o + 6], AF.Exp, scale=-0.5)],
                reads=[B_stat[tb]], writes=[B_stat[tb]])
            p.group("dve", [lambda e, o=o: e.scalar_tensor_tensor(
                stat[:, o + 7:o + 8], stat[:, o + 2:o + 3], -1.0, stat[:, o + 6:o + 7],
                ALU.mult, ALU.mult)],
                reads=[B_stat[tb]], writes=[B_stat[tb]])
            p.group("dve", [lambda e, sl=sl, o=o: e.tensor_scalar(
                xn_tm, sl, stat[:, o + 6:o + 7], stat[:, o + 7:o + 8],
                ALU.mult, ALU.add)],
                reads=slbufs + [B_stat[tb]], writes=[B_xn_tm])
            for c in range(NC_D):
                pb = c % 2
                sli = (c // 2) % 8
                p.group("pe", [lambda e, c=c, pb=pb, sli=sli: e.transpose(
                    ptr[pb][:, sli * 128:(sli + 1) * 128],
                    xn_tm[:, c * 128:(c + 1) * 128], idbf_sb)],
                    reads=[B_xn_tm, B_consts], writes=[B_ptr[pb]])
                dst = xnf_bf[:, c * 512 + tb * 128:c * 512 + (tb + 1) * 128]
                p.group("dve", [lambda e, dst=dst, pb=pb, sli=sli: e.tensor_copy(
                    dst, ptr[pb][:, sli * 128:(sli + 1) * 128])],
                    reads=[B_ptr[pb]],
                    writes=[B_xnf[c]] + accc(c * 1024 + tb * 256, 256))

        if t == 0:
            tap("xnf", xnf_bf, [B_xnf[c] for c in range(NC_D)] + accc(0, 16384))

        for dcg in range(8):
            slot = dcg % 2
            p.group("sync", [lambda e, dcg=dcg, slot=slot: e.dma_start(
                imgqw[slot], img_q[dcg])],
                reads=[B_scr_q[dcg]], writes=[B_imgqw[slot]], dma=True)
            if t == 0 and dcg == 0:
                tap("wqimg", imgqw[slot], [B_imgqw[slot]])
            for dd in range(2):
                dc = dcg * 2 + dd
                pb = dc % 2
                fns = [lambda e, fc=fc, dd=dd, slot=slot, pb=pb: e.matmul(
                    big[pb], imgqw[slot][:, fc * 256 + dd * 128:fc * 256 + (dd + 1) * 128],
                    xnf_bf[:, fc * 512:(fc + 1) * 512],
                    start=(fc == 0), stop=(fc == NC_D - 1)) for fc in range(NC_D)]
                p.group("pe", fns,
                        reads=[B_imgqw[slot]] + [B_xnf[fc] for fc in range(NC_D)]
                        + accc(0, 16384),
                        writes=[B_big[pb]])
                if DEBUG and t == 0 and dc == 1:
                    p.group("dve", [lambda e, pb=pb: e.tensor_copy(yspare, big[pb])],
                            reads=[B_big[pb]], writes=[B_ysp])
                    tap("q1", yspare, [B_ysp])
                p.group("act", [lambda e, pb=pb: e.activation(sigB, big[pb], AF.Exp)],
                        reads=[B_big[pb]], writes=[B_sigB])
                p.group("dve", [lambda e: e.tensor_scalar(
                    sigA, sigB, 1.0, None, ALU.min)],
                    reads=[B_sigB], writes=[B_sigA])
                p.group("dve", [lambda e, pb=pb, dc=dc: e.scalar_tensor_tensor(
                    arena_bf[:, dc * 512:(dc + 1) * 512], big[pb], 0.0, sigA,
                    ALU.max, ALU.add)],
                    reads=[B_big[pb], B_sigA], writes=arc(dc * 1024, 1024))
                if DEBUG and t == 0 and dc == 1:
                    tap("sigA1", sigA, [B_sigA])
                    tap("sigB1", sigB, [B_sigB])

        if t == 0:
            tap("sigma", arena_bf[:, 0:8192], arc(0, 16384))


        if t == 0:
            emit_g_pass(0)
            emit_g_pass(1)
            emit_ug_prep(0)

        yslot = {c: (("acc2", c) if c < 8 else ("x2y", c - 8)) for c in range(NC_D)}
        yfree = [("sigA", 0), ("ysp", 0)]

        def ytile_ap(so, bf=False):
            kind, off = so
            if kind == "fix":
                return y2_bf[:, off * 512:(off + 1) * 512]
            if kind == "acc2":
                return acc_f[:, 4096 + off * 512:4096 + (off + 1) * 512]
            if kind == "x2y":
                return x2_f[:, off * 512:(off + 1) * 512]
            t_ = (sigA if kind == "sigA" else yspare)
            tb_ = (sigA_bf if kind == "sigA" else yspare_bf)
            return tb_[:, 0:512] if bf else t_

        def ytile_bufs(so):
            kind, off = so
            if kind == "fix":
                return [B_y2[off]]
            if kind == "acc2":
                return accc(16384 + off * 2048, 2048)
            if kind == "x2y":
                return x2c(off * 2048, 2048)
            return [B_sigA if kind == "sigA" else B_ysp]

        for c in range(NC_D):
            so = yslot[c]
            p.group("dve", [lambda e, so=so, c=c: e.tensor_scalar(
                ytile_ap(so), xnf_bf[:, c * 512:(c + 1) * 512],
                su_fm[:, c:c + 1], None, ALU.mult)],
                reads=[B_xnf[c], B_consts] + accc(c * 1024, 1024),
                writes=ytile_bufs(so))
        eng_i = 0
        for st in range(4):
            bit = 1 << st
            last = (st == 3)
            for a in range(NC_D):
                if a & bit:
                    continue
                b = a | bit
                sa, sb_ = yslot[a], yslot[b]
                if last:
                    f1, f2 = ("fix", a), ("fix", b)
                else:
                    f1, f2 = yfree[0], yfree[1]
                e1 = ["dve", "dve", "gp"][eng_i % 3]; eng_i += 1
                e2 = ["dve", "dve", "gp"][eng_i % 3]; eng_i += 1
                p.group(e1, [lambda e, sa=sa, sb_=sb_, f1=f1, last=last: e.tensor_tensor(
                    ytile_ap(f1, bf=last), ytile_ap(sa), ytile_ap(sb_), ALU.add)],
                    reads=ytile_bufs(sa) + ytile_bufs(sb_), writes=ytile_bufs(f1))
                p.group(e2, [lambda e, sa=sa, sb_=sb_, f2=f2, last=last: e.tensor_tensor(
                    ytile_ap(f2, bf=last), ytile_ap(sa), ytile_ap(sb_), ALU.subtract)],
                    reads=ytile_bufs(sa) + ytile_bufs(sb_), writes=ytile_bufs(f2))
                yslot[a], yslot[b] = f1, f2
                if not last:
                    yfree = [sa, sb_]
        y1slot = dict(yslot)


        if t == 0:
            emit_g_pass(2)
            emit_ug_prep(1)

        for hh in range(H):
            fns = []
            for tb in range(TBN):
                fns.append(lambda e, hh=hh, tb=tb: e.matmul(
                    misc[:, hh * 16 + tb * 4:hh * 16 + (tb + 1) * 4],
                    arena_bf[:, hh * 512 + tb * 128:hh * 512 + (tb + 1) * 128],
                    mn_sb.rearrange("p (n h) -> p n h", h=H)[:, :, hh],
                    start=True, stop=True))
            p.group("pe", fns, reads=arc(hh * 1024, 1024) + [B_mem], writes=[B_misc])
        rta = x2_f[:, 0:256]
        rtb = x2_f[:, 256:512]
        p.group("dve", [lambda e: e.tensor_scalar(rta, misc[:, 0:256], EPS_ATT,
                                                  None, ALU.max)],
                reads=[B_misc], writes=x2c(0, 1024))
        p.group("dve", [lambda e: e.reciprocal(rtb, rta)],
                reads=x2c(0, 1024), writes=x2c(1024, 1024))
        p.group("dve", [lambda e: e.tensor_tensor(rta, rtb, wact4, ALU.mult)],
                reads=x2c(1024, 1024) + [B_consts], writes=x2c(0, 1024))
        for hh in range(H):
            if t == 0 and hh == 8:
                emit_g_pass(3)
            for tb in range(TBN):
                fns = []
                for n in range(NB):
                    ti = hh * NB + n
                    fns.append(lambda e, hh=hh, tb=tb, n=n, ti=ti: e.matmul(
                        big[tb][:, n * 128:(n + 1) * 128],
                        arena_bf[:, hh * 512 + tb * 128:hh * 512 + (tb + 1) * 128],
                        mem_sb[:, ti * 128:(ti + 1) * 128], start=True, stop=True))
                p.group("pe", fns, reads=arc(hh * 1024, 1024) + [B_mem],
                        writes=[B_big[tb]])
                rc = hh * 16 + tb * 4
                ceng = "dve" if t == 0 else "act"
                def _c0(e, tb=tb, rc=rc):
                    if isinstance(e, bass.BassScalarEngine):
                        return e.activation(ctA, big[tb][:, 0:128], AF.Copy,
                                            scale=x2_f[:, rc:rc + 1])
                    return e.tensor_scalar(ctA, big[tb][:, 0:128],
                                           x2_f[:, rc:rc + 1], None, ALU.mult)
                def _c2(e, tb=tb, rc=rc):
                    if isinstance(e, bass.BassScalarEngine):
                        return e.activation(ctB, big[tb][:, 256:384], AF.Copy,
                                            scale=x2_f[:, rc + 2:rc + 3])
                    return e.tensor_scalar(ctB, big[tb][:, 256:384],
                                           x2_f[:, rc + 2:rc + 3], None, ALU.mult)
                tc0 = p.group(ceng, [_c0],
                              reads=[B_big[tb]] + x2c(0, 1024), writes=[B_ctA])
                tc2 = p.group(ceng, [_c2],
                              reads=[B_big[tb]] + x2c(0, 1024), writes=[B_ctB])
                p.group("dve", [lambda e, tb=tb, rc=rc: e.scalar_tensor_tensor(
                    sigB[:, 0:128], big[tb][:, 128:256], x2_f[:, rc + 1:rc + 2],
                    ctA, ALU.mult, ALU.add)],
                    reads=[B_big[tb], B_ctA] + x2c(0, 1024), writes=[B_sigB],
                    extra=[tc2])
                p.group("dve", [lambda e, tb=tb, rc=rc: e.scalar_tensor_tensor(
                    sigB[:, 128:256], big[tb][:, 384:512], x2_f[:, rc + 3:rc + 4],
                    ctB, ALU.mult, ALU.add)],
                    reads=[B_big[tb], B_ctB] + x2c(0, 1024), writes=[B_sigB])
                p.group("dve", [lambda e: e.tensor_tensor(
                    ct_bf, sigB[:, 0:128], sigB[:, 128:256], ALU.add)],
                    reads=[B_sigB], writes=[B_ctbf])
                idx = hh * TBN + tb
                pb = idx % 2
                sli = (idx // 2) % 8
                p.group("pe", [lambda e, pb=pb, sli=sli: e.transpose(
                    ptr[pb][:, sli * 128:(sli + 1) * 128], ct_bf, idbf_sb)],
                    reads=[B_ctbf, B_consts], writes=[B_ptr[pb]])
                dst_off = 16384 + hh * 1024 + tb * 256
                p.group("dve", [lambda e, pb=pb, sli=sli, dst_off=dst_off: e.tensor_copy(
                    arena_bf[:, dst_off // 2:dst_off // 2 + 128],
                    ptr[pb][:, sli * 128:(sli + 1) * 128])],
                    reads=[B_ptr[pb]], writes=arc(dst_off, 256))

        if t == 0:
            tap("comb", arena_bf[:, 8192:16384], arc(16384, 16384))

        if t == 0:
            emit_g_pass(4)
            emit_ug_prep(2)
        for os in range(4):
            if t == 0 and os == 2:
                emit_g_pass(5)
            for hf in range(2):
                slot = hf
                p.group("sync", [lambda e, os=os, hf=hf, slot=slot: e.dma_start(
                    imgqw[slot], img_wo[os * 2 + hf])],
                    reads=[B_scr_wo[os * 2 + hf]], writes=[B_imgqw[slot]], dma=True)
            for tb in range(TBN):
                pb = (os * TBN + tb) % 2
                fns = []
                for cf in range(NC_D):
                    hf = cf // 8
                    k = cf % 8
                    lhs_off = 16384 + cf * 1024 + tb * 256
                    fns.append(lambda e, cf=cf, hf=hf, k=k, pb=pb, lhs_off=lhs_off: e.matmul(
                        big[pb], arena_bf[:, lhs_off // 2:lhs_off // 2 + 128],
                        imgqw[hf][:, k * 512:(k + 1) * 512],
                        start=(cf == 0), stop=(cf == NC_D - 1)))
                p.group("pe", fns,
                        reads=[B_imgqw[0], B_imgqw[1]] + arc(16384, 16384),
                        writes=[B_big[pb]])
                acc_off = tb * 8192 + os * 2048
                p.group("dve", [lambda e, pb=pb, tb=tb, os=os: e.tensor_copy(
                    acc_f[:, tb * 2048 + os * 512:tb * 2048 + (os + 1) * 512], big[pb])],
                    reads=[B_big[pb]], writes=accc(acc_off, 2048))

        if t == 0:
            tap("acc", acc_f, accc(0, 32768))
            emit_g_pass(6)
            emit_ug_prep(3)
            emit_g_pass(7)

        for c in range(NC_D):
            so = y1slot[c]
            pb = c % 2
            psum = [big[4], misc][pb]
            psb = [B_big[4], B_misc][pb]
            p.group("pe", [lambda e, so=so, psum=psum: e.matmul(
                psum, h128_sb, ytile_ap(so, bf=True), start=True, stop=True)],
                reads=ytile_bufs(so) + [B_consts], writes=[psb])
            p.group("act", [lambda e, c=c, psum=psum: e.activation(
                y2_bf[:, c * 512:(c + 1) * 512], psum, AF.Copy)],
                reads=[psb], writes=[B_y2[c]])

        if t == 0:
            tap("y2", y2_bf, [B_y2[c] for c in range(NC_D)])

        ug_slots = [imgug[0], imgug[1], imgqw[0]]
        ug_ubufs = [B_imgug[0], B_imgug[1], B_imgqw[0]]
        ug_gbufs = [B_imgug_g[0], B_imgug_g[1], B_imgqw_g]
        for ic in range(NC_I):
            slot = ic % 3
            sap = ug_slots[slot]
            p.group("sync", [lambda e, ic=ic, sap=sap: e.dma_start(
                sap[:, 0:2048], img_ug[ic, :, 0:2048])],
                reads=[B_scr_ug[ic]], writes=[ug_ubufs[slot]], dma=True)
            p.group("sync", [lambda e, ic=ic, sap=sap: e.dma_start(
                sap[:, 2048:4096], img_ug[ic, :, 2048:4096])],
                reads=[B_scr_ug[ic]], writes=[ug_gbufs[slot]], dma=True)
            if t == 0 and ic == 0:
                tap("ugimg", sap, [ug_ubufs[slot], ug_gbufs[slot]])
            ub = [big[2], big[0]][ic % 2]
            gb = [big[3], big[1]][ic % 2]
            ubf = [B_big[2], B_big[0]][ic % 2]
            gbf = [B_big[3], B_big[1]][ic % 2]
            fns = [lambda e, fc=fc, sap=sap, ub=ub: e.matmul(
                ub, sap[:, fc * 128:(fc + 1) * 128],
                y2_bf[:, fc * 512:(fc + 1) * 512],
                start=(fc == 0), stop=(fc == NC_D - 1)) for fc in range(NC_D)]
            p.group("pe", fns,
                    reads=[ug_ubufs[slot]] + [B_y2[c] for c in range(NC_D)],
                    writes=[ubf])
            fns = [lambda e, fc=fc, sap=sap, gb=gb: e.matmul(
                gb, sap[:, 2048 + fc * 128:2048 + (fc + 1) * 128],
                y2_bf[:, fc * 512:(fc + 1) * 512],
                start=(fc == 0), stop=(fc == NC_D - 1)) for fc in range(NC_D)]
            p.group("pe", fns,
                    reads=[ug_gbufs[slot]] + [B_y2[c] for c in range(NC_D)],
                    writes=[gbf])
            p.group("act", [lambda e, ub=ub, ic=ic: e.activation(
                usb, ub, AF.Copy, scale=sclu_fm[:, ic:ic + 1])],
                reads=[ubf, B_consts], writes=[B_usb])
            p.group("act", [lambda e, gb=gb, ic=ic: e.activation(
                gsb, gb, AF.Silu, scale=sclg_fm[:, ic:ic + 1])],
                reads=[gbf, B_consts], writes=[B_gsb])
            p.group("dve", [lambda e, ic=ic: e.tensor_tensor(
                arena_bf[:, ic * 512:(ic + 1) * 512], usb, gsb, ALU.mult)],
                reads=[B_usb, B_gsb], writes=arc(ic * 1024, 1024))

        if t == 0:
            tap("zp", arena_bf, sum((arc(ic * 1024, 1024) for ic in range(NC_I)), []))

        if t + 1 < NT:
            for tb in (0, 1):
                sl, slbufs = xslot(t + 1, tb)
                p.group("sync", [lambda e, t=t, tb=tb, sl=sl: e.dma_start(
                    sl, x_d[(t + 1) * TS + tb * 128:(t + 1) * TS + (tb + 1) * 128, :])],
                    writes=slbufs, dma=True)
        gd_slots = [imggd[0], imggd[1], imgqw[1][:, 0:2048]]
        gd_bufs = [B_imggd[0], B_imggd[1], B_imgqw[1]]
        for os in range(4):
            bk = [tb for tb in range(TBN)]
            for grp in range(16):
                slot = (os * 16 + grp) % 2
                sap = gd_slots[slot]
                p.group("sync", [lambda e, os=os, grp=grp, sap=sap: e.dma_start(
                    sap.rearrange("p (g c) -> p g c", g=4),
                    scr_g[os, 4 * grp:4 * grp + 4].rearrange("g p c -> p g c"))],
                    reads=[B_scr_g[os]], writes=[gd_bufs[slot]], dma=True)
                if t == 0 and os == 0 and grp == 0:
                    tap("gdimg", sap, [gd_bufs[slot]])
                for g in range(4):
                    ic = grp * 4 + g
                    fns = [lambda e, ic=ic, g=g, sap=sap, tb=tb, bk=bk: e.matmul(
                        big[bk[tb]], arena_bf[:, ic * 512 + tb * 128:ic * 512 + (tb + 1) * 128],
                        sap[:, g * 512:(g + 1) * 512],
                        start=(ic == 0), stop=(ic == NC_I - 1)) for tb in range(TBN)]
                    p.group("pe", fns,
                            reads=[gd_bufs[slot]] + arc(ic * 1024, 1024),
                            writes=[B_big[bk[tb]] for tb in range(TBN)])
            for tb in range(TBN):
                sslot = os % 2
                soff = sslot * 8192 + tb * 2048
                p.group("dve", [lambda e, tb=tb, os=os, sslot=sslot, bk=bk: e.scalar_tensor_tensor(
                    x2_f[:, sslot * 2048 + tb * 512:sslot * 2048 + (tb + 1) * 512],
                    big[bk[tb]], 1.0,
                    acc_f[:, tb * 2048 + os * 512:tb * 2048 + (os + 1) * 512],
                    ALU.bypass, ALU.add)],
                    reads=[B_big[bk[tb]]] + accc(tb * 8192 + os * 2048, 2048),
                    writes=x2c(soff, 2048))
                p.group("gpd", [lambda e, tb=tb, os=os, sslot=sslot, row0=row0: e.dma_start(
                    out_d[row0 + tb * 128:row0 + (tb + 1) * 128, os * 512:(os + 1) * 512],
                    x2_f[:, sslot * 2048 + tb * 512:sslot * 2048 + (tb + 1) * 512],
                    accum_op=ALU.add)],
                    reads=x2c(soff, 2048), writes=[B_outc[t][tb * 4 + os]], dma=True)
        if t + 1 < NT:
            for tb in (2, 3):
                sl, slbufs = xslot(t + 1, tb)
                p.group("sync", [lambda e, t=t, tb=tb, sl=sl: e.dma_start(
                    sl, x_d[(t + 1) * TS + tb * 128:(t + 1) * TS + (tb + 1) * 128, :])],
                    writes=slbufs, dma=True)

    semname = {k: k for k in STREAM}
    with (
        nc.semaphore("s_pe") as s_pe,
        nc.semaphore("s_dve") as s_dve,
        nc.semaphore("s_act") as s_act,
        nc.semaphore("s_actd") as s_actd,
        nc.semaphore("s_gp") as s_gp,
        nc.semaphore("s_gpd") as s_gpd,
        nc.semaphore("s_sync") as s_sync,
        nc.Block() as block,
    ):
        sems = {"pe": s_pe, "dve": s_dve, "act": s_act, "actd": s_actd,
                "gp": s_gp, "gpd": s_gpd, "sync": s_sync}

        def replay(eng, stream):
            for waits, fns, key, dma in p.ops[stream]:
                for sk, v in waits:
                    eng.wait_ge(sems[sk], v)
                if dma:
                    for fn in fns:
                        fn(eng).then_inc(sems[key], 16)
                else:
                    insts = [fn(eng) for fn in fns]
                    insts[-1].then_inc(sems[key], 1)

        @block.tensor
        def _(e):
            replay(e, "pe")

        @block.vector
        def _(e):
            replay(e, "dve")

        @block.scalar
        def _(e):
            replay(e, "act")

        @block.gpsimd
        def _(e):
            replay(e, "gp")
            e.wait_ge(sems["gpd"], p.cnt["gpd"])

        @block.sync
        def _(e):
            replay(e, "sync")
            e.wait_ge(sems["gpd"], p.cnt["gpd"])
            e.wait_ge(sems["actd"], p.cnt["actd"])

    return nc


_cache = {}


def kernel(**inputs):
    if "nc" not in _cache:
        _cache["nc"] = build_program()
    nc = _cache["nc"]

    h128 = hadamard(128).astype(ml_dtypes.bfloat16)
    idbf = np.eye(128).astype(ml_dtypes.bfloat16)
    idf = np.eye(128, dtype=np.float32)
    nsel = np.zeros((64, NB), dtype=np.float32)
    for n in range(NB):
        for hh in range(H):
            nsel[n * H + hh, n] = 1.0

    x = np.ascontiguousarray(inputs["L0_attn_hidden_states"])
    common = {
        "wq": np.ascontiguousarray(inputs["L0_attn_Wq"]),
        "wo": np.ascontiguousarray(inputs["L0_attn_Wo"]),
        "bw": np.ascontiguousarray(inputs["L0_attn_bank_weights"]),
        "mem": np.ascontiguousarray(inputs["L0_attn_memories"]),
        "mn": np.ascontiguousarray(inputs["L0_attn_memory_norms"]),
        "suug": np.ascontiguousarray(inputs["L0_ffn_SU_ug"]),
        "sudp": np.ascontiguousarray(inputs["L0_ffn_SU_dp"]),
        "wsug": np.ascontiguousarray(inputs["L0_ffn_Wscale_ug"]),
        "wsdp": np.ascontiguousarray(inputs["L0_ffn_Wscale_dp"]),
        "wup": np.ascontiguousarray(inputs["L0_ffn_W_up"]),
        "wgate": np.ascontiguousarray(inputs["L0_ffn_W_gate"]),
        "wdown": np.ascontiguousarray(inputs["L0_ffn_W_down"]),
        "h128": h128, "idbf": idbf, "idf": idf, "nsel": nsel,
    }
    in_maps = [dict(common, x=np.ascontiguousarray(x[c])) for c in range(NCORES)]
    res = run_bass_kernel_spmd(nc, in_maps, core_ids=list(range(NCORES)))
    out = np.stack([res.results[c]["out"] for c in range(NCORES)], axis=0)
    return out.astype(np.float32)


# version 13
# speedup vs baseline: 1.1677x; 1.1677x over previous
import numpy as np
import ml_dtypes
import concourse.bass as bass
import concourse.mybir as mybir
from concourse.bass_utils import run_bass_kernel_spmd

F32 = mybir.dt.float32
BF16 = mybir.dt.bfloat16
AF = mybir.ActivationFunctionType
ALU = mybir.AluOpType
AX = mybir.AxisListType

NCORES = 8
S = 4096
D = 2048
H = 16
NB = 4
II = 8192
TS = 512
NT = S // TS
TBN = TS // 128
NC_D = D // 128
NC_I = II // 128
EPS_LN = 1e-5
EPS_ATT = 1e-6
DEBUG = False
LATE_Y = True

STREAM = {"pe": "pe", "dve": "dve", "act": "act", "actd": "act",
          "gp": "gp", "gpd": "gp", "sync": "sync"}
ENGSTREAMS = ("pe", "dve", "act", "gp", "sync")


class Buf:
    __slots__ = ("name", "w", "r", "frozen")

    def __init__(self, name):
        self.name = name
        self.w = None
        self.r = []
        self.frozen = False


class Prog:
    def __init__(self):
        self.ops = {e: [] for e in ENGSTREAMS}
        self.cnt = {k: 0 for k in STREAM}
        self.maxwait = {e: {} for e in ENGSTREAMS}

    def group(self, key, fns, reads=(), writes=(), dma=False, extra=()):
        stream = STREAM[key]
        waits = {}

        def addtok(t):
            if t is not None:
                waits[t[0]] = max(waits.get(t[0], 0), t[1])
        for b in reads:
            addtok(b.w)
        for b in writes:
            addtok(b.w)
            for t in b.r:
                addtok(t)
        for t in extra:
            addtok(t)
        mw = self.maxwait[stream]
        wl = []
        for sk in sorted(waits):
            v = waits[sk]
            if mw.get(sk, 0) >= v:
                continue
            wl.append((sk, v))
            mw[sk] = v
        inc = 16 * len(fns) if dma else 1
        self.cnt[key] += inc
        tok = (key, self.cnt[key])
        self.ops[stream].append((wl, fns, key, dma))
        for b in writes:
            b.w = tok
            b.r = []
        for b in reads:
            if not b.frozen:
                b.r.append(tok)
        return tok


def hadamard(n):
    h = np.array([[1.0]])
    while h.shape[0] < n:
        h = np.block([[h, h], [h, -h]])
    return h


def build_program():
    nc = bass.Bass("TRN2", target_bir_lowering=False)

    x_d = nc.dram_tensor("x", [S, D], F32, kind="ExternalInput").ap()
    wq_d = nc.dram_tensor("wq", [D, D], F32, kind="ExternalInput").ap()
    wo_d = nc.dram_tensor("wo", [D, D], F32, kind="ExternalInput").ap()
    bw_d = nc.dram_tensor("bw", [H, NB], F32, kind="ExternalInput").ap()
    mem_d = nc.dram_tensor("mem", [NB, H, 128, 128], F32, kind="ExternalInput").ap()
    mn_d = nc.dram_tensor("mn", [NB, H, 128], F32, kind="ExternalInput").ap()
    suug_d = nc.dram_tensor("suug", [D], F32, kind="ExternalInput").ap()
    sudp_d = nc.dram_tensor("sudp", [II], F32, kind="ExternalInput").ap()
    wsug_d = nc.dram_tensor("wsug", [2 * II], F32, kind="ExternalInput").ap()
    wsdp_d = nc.dram_tensor("wsdp", [D], F32, kind="ExternalInput").ap()
    wup_d = nc.dram_tensor("wup", [II, D], F32, kind="ExternalInput").ap()
    wgate_d = nc.dram_tensor("wgate", [II, D], F32, kind="ExternalInput").ap()
    wdown_d = nc.dram_tensor("wdown", [D, II], F32, kind="ExternalInput").ap()
    h128_d = nc.dram_tensor("h128", [128, 128], BF16, kind="ExternalInput").ap()
    idbf_d = nc.dram_tensor("idbf", [128, 128], BF16, kind="ExternalInput").ap()
    idf_d = nc.dram_tensor("idf", [128, 128], F32, kind="ExternalInput").ap()
    nsel_d = nc.dram_tensor("nsel", [64, NB], F32, kind="ExternalInput").ap()
    out_d = nc.dram_tensor("out", [S, D], F32, kind="ExternalOutput").ap()

    wq_nat = nc.dram_tensor("wq_nat", [D, D], BF16).ap()
    wo_nat = nc.dram_tensor("wo_nat", [D, D], BF16).ap()
    wup_nat = nc.dram_tensor("wup_nat", [II, D], BF16).ap()
    wgate_nat = nc.dram_tensor("wgate_nat", [II, D], BF16).ap()
    wd_nat = nc.dram_tensor("wd_nat", [D, II], BF16).ap()
    img_q = nc.dram_tensor("img_q", [8, 128, 4096], BF16).ap()
    img_wo = nc.dram_tensor("img_wo", [8, 128, 4096], BF16).ap()
    img_ug = nc.dram_tensor("img_ug", [NC_I, 128, 4096], BF16).ap()
    scr_g = nc.dram_tensor("scr_g", [4, NC_I, 128, 512], BF16).ap()
    wdT_nat = nc.dram_tensor("wdT_nat", [II, D], BF16).ap()
    smallscr = nc.dram_tensor("smallscr", [64], F32).ap()

    RESERVE = 212000
    _resv = nc.alloc_sbuf_tensor("kv_arena_reserve", [128, RESERVE], mybir.dt.uint8)
    SBUF_BASE = nc.lookup_mloc(_resv).addr

    def at(name, shape, dtype, off):
        return nc.alloc_sbuf_tensor_at(name, shape, dtype, offset=SBUF_BASE + off).ap()

    OFF_ARENA = 0
    arena_bf = at("arena_bf", [128, 32768], BF16, OFF_ARENA)
    y0_f = at("y0_f", [128, 8192], F32, OFF_ARENA)
    arena_hi_f = at("arena_hi_f", [128, 8192], F32, OFF_ARENA + 32768)
    OFF_ACC = 65536
    acc_f = at("acc_f", [128, 8192], F32, OFF_ACC)
    xnf_bf = at("xnf_bf", [128, 8192], BF16, OFF_ACC)
    OFF_X2 = 98304
    x2_f = at("x2_f", [128, 4096], F32, OFF_X2)
    y2_bf = at("y2_bf", [128, 8192], BF16, 114688)
    xld_f = at("xld_f", [128, 4096], F32, 114688)
    xn_tm = at("xn_tm", [128, 2048], BF16, 131072)
    imgqw = [at(f"imgqw{i}", [128, 4096], BF16, 135168 + i * 8192) for i in range(2)]
    imgug = [at(f"imgug{i}", [128, 4096], BF16, 151552 + i * 8192) for i in range(2)]
    imggd = [at(f"imggd{i}", [128, 2048], BF16, 167936 + i * 4096) for i in range(2)]
    wsbc_f = at("wsbc_f", [128, 2048], F32, 167936)
    mem_sb = at("mem_sb", [128, 8192], BF16, 176128)

    base = [192512]

    def nx(name, shape, dtype):
        t = at(name, shape, dtype, base[0])
        sz = shape[1] * (2 if dtype == BF16 else 4)
        base[0] += (sz + 31) // 32 * 32
        return t
    mn_sb = nx("mn_sb", [128, 64], BF16)
    h128_sb = nx("h128_sb", [128, 128], BF16)
    idbf_sb = nx("idbf_sb", [128, 128], BF16)
    idf_sb = nx("idf_sb", [128, 128], F32)
    ones1_f = nx("ones1_f", [1, 128], F32)
    onebf = nx("onebf", [128, 1], BF16)
    wact4 = nx("wact4", [128, 256], F32)
    su_fm = nx("su_fm", [128, 16], F32)
    sclu_fm = nx("sclu_fm", [128, 64], F32)
    sclg_fm = nx("sclg_fm", [128, 64], F32)
    sudp_fm = nx("sudp_fm", [128, 64], F32)
    off_sigA = base[0]
    sigA = nx("sigA", [128, 512], F32)
    sigA_bf = at("sigA_bf", [128, 1024], BF16, off_sigA)
    sigB = nx("sigB", [128, 512], F32)
    ctA = nx("ctA", [128, 128], F32)
    ctB = nx("ctB", [128, 128], F32)
    ct_bf = nx("ct_bf", [128, 128], BF16)
    off_ysp = base[0]
    yspare = nx("yspare", [128, 512], F32)
    yspare_bf = at("yspare_bf", [128, 1024], BF16, off_ysp)
    stat = nx("stat", [128, 32], F32)
    bwt = nx("bwt", [16, 32], F32)
    nsel_sb = nx("nsel_sb", [64, NB], F32)
    mnrow = nx("mnrow", [64, 128], BF16)
    vrow = nx("vrow", [64, 128], F32)
    wsrow = nx("wsrow", [1, 512], F32)
    d64 = nx("d64", [64, 4], F32)
    epsln = nx("epsln", [128, 1], F32)
    usb = nx("usb", [128, 512], BF16)
    gsb = nx("gsb", [128, 512], BF16)
    stg = [nx(f"stg{i}", [128, 512], BF16) for i in range(2)]
    assert base[0] <= RESERVE, f"sbuf overflow {base[0]}"

    big = [nc.alloc_psum_tensor(f"big{i}", [128, 512], F32).ap() for i in range(5)]
    ptr = [nc.alloc_psum_tensor(f"ptr{i}", [128, 1024], BF16).ap() for i in range(2)]
    misc = nc.alloc_psum_tensor("misc", [128, 512], F32).ap()
    assert nc.psum_banks_remaining == 0, nc.psum_banks_remaining

    p = Prog()

    AR = [Buf(f"ar{i}") for i in range(128)]

    def arc(byte0, nbytes):
        return AR[byte0 // 512:(byte0 + nbytes + 511) // 512]
    ACC = [Buf(f"acc{i}") for i in range(64)]

    def accc(byte0, nbytes):
        return ACC[byte0 // 512:(byte0 + nbytes + 511) // 512]
    X2C = [Buf(f"x2c{i}") for i in range(32)]

    def x2c(byte0, nbytes):
        return X2C[byte0 // 512:(byte0 + nbytes + 511) // 512]
    B_big = [Buf(f"pbig{i}") for i in range(5)]
    B_ptr = [Buf(f"pptr{i}") for i in range(2)]
    B_misc = Buf("pmisc")
    B_xn_tm = Buf("xn_tm")
    B_xnf = [Buf(f"xnf{c}") for c in range(NC_D)]
    B_y2 = [Buf(f"y2_{c}") for c in range(NC_D)]
    B_imgqw = [Buf("imgqw0"), Buf("imgqw1")]
    B_imgug = [Buf("imgug0"), Buf("imgug1")]
    B_imgug_g = [Buf("imgug0g"), Buf("imgug1g")]
    B_imgqw_g = Buf("imgqw0g")
    B_imggd = [Buf("imggd0"), Buf("imggd1")]
    B_stat = [Buf(f"stat{tb}") for tb in range(TBN)]
    B_rt = Buf("rt")
    B_sigA = Buf("sigA")
    B_sigB = Buf("sigB")
    B_ctA = Buf("ctA")
    B_ctB = Buf("ctB")
    B_ctbf = Buf("ct_bf")
    B_ct4 = [Buf(f"ct4_{i}") for i in range(4)]
    B_ysp = Buf("yspare")
    B_usb = Buf("usb")
    B_gsb = Buf("gsb")
    B_stg = [Buf("stg0"), Buf("stg1")]
    B_misc_sb = Buf("misc_sb")
    B_mem = Buf("mem_sb")
    B_consts = Buf("consts")
    B_scr_q = [Buf(f"scrq{i}") for i in range(8)]
    B_scr_wo = [Buf(f"scrwo{i}") for i in range(8)]
    B_scr_ug = [Buf(f"scrug{i}") for i in range(NC_I)]
    B_scr_g = [Buf(f"scrg{i}") for i in range(4)]
    B_scr_sm = Buf("scr_small")
    B_scr_wdT = Buf("scr_wdT")
    B_outc = [[Buf(f"out{t}_{k}") for k in range(17)] for t in range(NT)]

    def xslot(t, tb):
        if tb >= 2:
            k = tb - 2
            return x2_f[:, k * 2048:(k + 1) * 2048], x2c(k * 8192, 8192)
        k = tb
        return xld_f[:, k * 2048:(k + 1) * 2048], list(B_y2[8 * k:8 * k + 8])
    for tb in range(TBN):
        sl, bufs = xslot(0, tb)
        p.group("sync", [lambda e, tb=tb, sl=sl: e.dma_start(
            sl, x_d[tb * 128:(tb + 1) * 128, :])], writes=bufs, dma=True)
    p.group("sync", [lambda e: e.dma_start(h128_sb, h128_d),
                     lambda e: e.dma_start(idbf_sb, idbf_d),
                     lambda e: e.dma_start(idf_sb, idf_d),
                     lambda e: e.dma_start(nsel_sb, nsel_d)],
            writes=[B_consts], dma=True)
    p.group("dve", [lambda e: e.memset(ones1_f, 1.0)], writes=[B_consts])
    p.group("dve", [lambda e: e.memset(onebf, 1.0)], writes=[B_consts])
    p.group("dve", [lambda e: e.memset(epsln, EPS_LN)], writes=[B_consts])

    cast_toks = {}

    def cast(name, dst, src, nchunks):
        rows = src.shape[0]
        cr = rows // nchunks
        cast_toks[name] = [
            p.group("gpd", [lambda e, k=k: e.dma_start(dst[k * cr:(k + 1) * cr, :],
                                                       src[k * cr:(k + 1) * cr, :])],
                    dma=True) for k in range(nchunks)]
        cast_toks[name + "_cr"] = cr
    def cast_tok(name, row0, nrows):
        cr = cast_toks[name + "_cr"]
        assert row0 // cr == (row0 + nrows - 1) // cr
        return cast_toks[name][row0 // cr]

    memfns = []
    for hh in range(H):
        for n in range(NB):
            ti = hh * NB + n
            memfns.append(lambda e, n=n, hh=hh, ti=ti: e.dma_start(
                mem_sb[:, ti * 128:(ti + 1) * 128], mem_d[n, hh]))
    p.group("gpd", memfns, writes=[B_mem], dma=True)
    p.group("gpd", [lambda e: e.dma_start(mnrow, mn_d.rearrange("n h d -> (n h) d"))],
            writes=[B_misc_sb], dma=True)
    p.group("pe", [lambda e: e.transpose(ptr[0][:, 0:64], mnrow, idbf_sb[0:64, 0:64])],
            reads=[B_misc_sb, B_consts], writes=[B_ptr[0]])
    p.group("dve", [lambda e: e.tensor_copy(mn_sb, ptr[0][:, 0:64])],
            reads=[B_ptr[0]], writes=[B_mem])

    def vec_fm(dst, src_ap, rows):
        p.group("gpd", [lambda e: e.dma_start(vrow[0:rows, :], src_ap)],
                writes=[B_misc_sb], dma=True)
        p.group("pe", [lambda e: e.transpose(misc[:, 64:64 + rows], vrow[0:rows, :],
                                             idf_sb[0:rows, 0:rows])],
                reads=[B_misc_sb, B_consts], writes=[B_misc])
        p.group("dve", [lambda e: e.tensor_copy(dst, misc[:, 64:64 + rows])],
                reads=[B_misc], writes=[B_consts])
    vec_fm(su_fm, suug_d.rearrange("(a b) -> a b", b=128), 16)
    vec_fm(sclu_fm, wsug_d[0:II].rearrange("(a b) -> a b", b=128), 64)
    vec_fm(sclg_fm, wsug_d[II:2 * II].rearrange("(a b) -> a b", b=128), 64)
    vec_fm(sudp_fm, sudp_d.rearrange("(a b) -> a b", b=128), 64)
    p.group("dve", [lambda e: e.tensor_mul(sclu_fm, sclu_fm, sudp_fm)],
            reads=[B_consts], writes=[B_consts])
    p.group("dve", [lambda e: e.tensor_scalar_mul(sclu_fm, sclu_fm, D ** -0.5)],
            reads=[B_consts], writes=[B_consts])
    p.group("dve", [lambda e: e.tensor_scalar_mul(sclg_fm, sclg_fm, D ** -0.5)],
            reads=[B_consts], writes=[B_consts])

    for os in range(4):
        p.group("gpd", [lambda e, os=os: e.dma_start(
            wsrow, wsdp_d.rearrange("(a b) -> a b", b=512)[os:os + 1, :])],
            writes=[B_misc_sb], dma=True)
        p.group("pe", [lambda e: e.matmul(big[4], ones1_f, wsrow, start=True, stop=True)],
                reads=[B_misc_sb, B_consts], writes=[B_big[4]])
        p.group("dve", [lambda e, os=os: e.tensor_scalar(
            wsbc_f[:, os * 512:(os + 1) * 512], big[4], II ** -0.5, None, ALU.mult)],
            reads=[B_big[4]], writes=[B_imggd[0], B_imggd[1]])

    p.group("gpd", [lambda e: e.dma_start(bwt[0:16, 0:4], bw_d)], writes=[B_misc_sb], dma=True)
    p.group("dve", [lambda e: e.reduce_max(bwt[0:16, 4:5], bwt[0:16, 0:4], axis=AX.X)],
            reads=[B_misc_sb], writes=[B_misc_sb])
    p.group("dve", [lambda e: e.tensor_scalar(bwt[0:16, 8:12], bwt[0:16, 0:4],
                                              bwt[0:16, 4:5], None, ALU.subtract)],
            reads=[B_misc_sb], writes=[B_misc_sb])
    p.group("act", [lambda e: e.activation(bwt[0:16, 12:16], bwt[0:16, 8:12], AF.Exp)],
            reads=[B_misc_sb], writes=[B_misc_sb])
    p.group("dve", [lambda e: e.reduce_sum(bwt[0:16, 16:17], bwt[0:16, 12:16], axis=AX.X)],
            reads=[B_misc_sb], writes=[B_misc_sb])
    p.group("dve", [lambda e: e.reciprocal(bwt[0:16, 20:21], bwt[0:16, 16:17])],
            reads=[B_misc_sb], writes=[B_misc_sb])
    p.group("dve", [lambda e: e.tensor_scalar(bwt[0:16, 24:28], bwt[0:16, 12:16],
                                              bwt[0:16, 20:21], None, ALU.mult)],
            reads=[B_misc_sb], writes=[B_misc_sb])
    p.group("pe", [lambda e: e.matmul(misc[0:64, 128:129], mn_sb, onebf, start=True, stop=True)],
            reads=[B_mem, B_consts], writes=[B_misc])
    p.group("dve", [lambda e: e.tensor_copy(d64[:, 0:1], misc[0:64, 128:129])],
            reads=[B_misc], writes=[B_misc_sb])
    p.group("pe", [lambda e: e.matmul(misc[0:1, 160:164], d64[:, 0:1], nsel_sb,
                                      start=True, stop=True)],
            reads=[B_misc_sb, B_consts], writes=[B_misc])
    p.group("dve", [lambda e: e.tensor_single_scalar(d64[0:1, 0:4], misc[0:1, 160:164],
                                                     EPS_ATT, ALU.is_ge)],
            reads=[B_misc], writes=[B_misc_sb])
    p.group("pe", [lambda e: e.matmul(misc[0:16, 192:196], ones1_f[0:1, 0:16],
                                      d64[0:1, 0:4], start=True, stop=True)],
            reads=[B_misc_sb, B_consts], writes=[B_misc])
    p.group("dve", [lambda e: e.tensor_tensor(bwt[0:16, 28:32], bwt[0:16, 24:28],
                                              misc[0:16, 192:196], ALU.mult)],
            reads=[B_misc, B_misc_sb], writes=[B_misc_sb])
    p.group("gpd", [lambda e: e.dma_start(smallscr.rearrange("(a b) -> a b", b=4),
                                          bwt[0:16, 28:32])],
            reads=[B_misc_sb], writes=[B_scr_sm], dma=True)
    for tb in range(TBN):
        p.group("gpd", [lambda e, tb=tb: e.dma_start(
            wsrow[:, 0:256].rearrange("o (h r) -> o h r", r=16)[:, :, tb * 4:(tb + 1) * 4],
            smallscr.rearrange("(o h n) -> o h n", o=1, n=4))],
            reads=[B_scr_sm], writes=[B_misc_sb], dma=True)
    p.group("pe", [lambda e: e.matmul(misc[:, 224:480], ones1_f, wsrow[0:1, 0:256],
                                      start=True, stop=True)],
            reads=[B_misc_sb, B_consts], writes=[B_misc])
    p.group("dve", [lambda e: e.tensor_copy(wact4, misc[:, 224:480])],
            reads=[B_misc], writes=[B_consts])

    crd = D // 4
    rows4 = II // 4
    cast_toks["wd"] = []
    cast_toks["wd_cr"] = crd
    cast_toks["wup"] = []
    cast_toks["wgate"] = []
    cast_toks["wup_cr"] = rows4
    cast_toks["wgate_cr"] = rows4

    def wd_chunk(k):
        cast_toks["wd"].append(p.group("gpd", [
            lambda e, k=k: e.dma_start(wd_nat[k * crd:(k + 1) * crd, :],
                                       wdown_d[k * crd:(k + 1) * crd, :])], dma=True))

    def ug_chunk(k):
        cast_toks["wup"].append(p.group("gpd", [
            lambda e, k=k: e.dma_start(wup_nat[k * rows4:(k + 1) * rows4, :],
                                       wup_d[k * rows4:(k + 1) * rows4, :])], dma=True))
        cast_toks["wgate"].append(p.group("gpd", [
            lambda e, k=k: e.dma_start(wgate_nat[k * rows4:(k + 1) * rows4, :],
                                       wgate_d[k * rows4:(k + 1) * rows4, :])], dma=True))

    cast("wq", wq_nat, wq_d, 1)
    def emit_wq_prep():
        for fc in range(NC_D):
            slot = fc % 2
            p.group("actd", [lambda e, fc=fc, slot=slot: e.dma_start(
                imgqw[slot][:, 0:2048],
                wq_nat[:, fc * 128:(fc + 1) * 128], transpose=True)],
                writes=[B_imgqw[slot]], dma=True, extra=[cast_tok("wq", 0, D)])
            p.group("gpd", [lambda e, fc=fc, slot=slot: e.dma_start(
                img_q[:, :, fc * 256:(fc + 1) * 256].rearrange("g p c -> p g c"),
                imgqw[slot][:, 0:2048].rearrange("p (g c) -> p g c", g=8))],
                reads=[B_imgqw[slot]], writes=[B_scr_q[dcg] for dcg in range(8)],
                dma=True)

    cast("wo", wo_nat, wo_d, 1)
    def emit_wo_prep():
        for cf in range(NC_D):
            slot = cf % 2
            hf = cf // 8
            k = cf % 8
            p.group("actd", [lambda e, cf=cf, slot=slot: e.dma_start(
                imgqw[slot][:, 0:2048],
                wo_nat[:, cf * 128:(cf + 1) * 128], transpose=True)],
                writes=[B_imgqw[slot]], dma=True, extra=[cast_tok("wo", 0, D)])
            p.group("gpd", [lambda e, hf=hf, k=k, slot=slot: e.dma_start(
                img_wo[hf::2, :, k * 512:(k + 1) * 512].rearrange("g p c -> p g c"),
                imgqw[slot][:, 0:2048].rearrange("p (g c) -> p g c", g=4))],
                reads=[B_imgqw[slot]],
                writes=[B_scr_wo[2 * os + hf] for os in range(4)], dma=True)

    def emit_ug_prep(blk):
            for w_i, (nat, coloff) in enumerate(((wup_nat, 0), (wgate_nat, 2048))):
                nm = "wup" if w_i == 0 else "wgate"
                for fc in range(NC_D):
                    slot = (blk * NC_D * 2 + w_i * NC_D + fc) % 2
                    p.group("actd", [lambda e, blk=blk, fc=fc, slot=slot, nat=nat: e.dma_start(
                        imgqw[slot][:, 0:2048],
                        nat[blk * 2048:(blk + 1) * 2048, fc * 128:(fc + 1) * 128],
                        transpose=True)],
                        writes=[B_imgqw[slot]], dma=True,
                        extra=[cast_tok(nm, blk * 2048, 2048)])
                    p.group("gpd", [lambda e, blk=blk, fc=fc, slot=slot, coloff=coloff: e.dma_start(
                        img_ug[16 * blk:16 * blk + 16, :, coloff + fc * 128:coloff + (fc + 1) * 128]
                        .rearrange("g p c -> p g c"),
                        imgqw[slot][:, 0:2048].rearrange("p (g c) -> p g c", g=16))],
                        reads=[B_imgqw[slot]],
                        writes=[B_scr_ug[16 * blk + g] for g in range(16)], dma=True)

    def emit_wdT_prep(half):
        for ib in range(32 * half, 32 * (half + 1)):
            slot = ib % 2
            p.group("actd", [lambda e, ib=ib, slot=slot: e.dma_start(
                imgqw[slot][:, 0:2048],
                wd_nat[:, ib * 128:(ib + 1) * 128], transpose=True)],
                writes=[B_imgqw[slot]], dma=True,
                extra=[cast_toks["wd"][3]])
            p.group("gpd", [lambda e, ib=ib, slot=slot: e.dma_start(
                wdT_nat[ib * 128:(ib + 1) * 128, :], imgqw[slot][:, 0:2048])],
                reads=[B_imgqw[slot]], writes=[B_scr_wdT], dma=True)

    emit_wq_prep()
    wd_chunk(0)
    wd_chunk(1)
    wd_chunk(2)
    wd_chunk(3)
    emit_wdT_prep(0)
    emit_wo_prep()
    emit_wdT_prep(1)
    ug_chunk(0)
    ug_chunk(1)
    ug_chunk(2)
    ug_chunk(3)

    def gtile_ap(so, width_bf=256):
        kind, off = so
        if kind == "ar":
            return arena_bf[:, off // 2: off // 2 + width_bf]
        if kind == "sigA":
            return sigA_bf[:, 0:width_bf]
        return yspare_bf[:, 0:width_bf]

    def gtile_bufs(so, width_bf=256):
        kind, off = so
        if kind == "ar":
            return arc(off, width_bf * 2)
        return [B_sigA if kind == "sigA" else B_ysp]

    def emit_g_pass(pp):
        os4 = pp // 2
        half = pp % 2
        gslot = {a: ("ar", 32768 + a * 512) for a in range(64)}
        p.group("sync", [lambda e, pp=pp: e.dma_start(
            arena_bf[:, 16384:32768].rearrange("p (g c) -> p g c", c=256),
            wdT_nat[:, pp * 256:(pp + 1) * 256].rearrange("(g p) c -> p g c", p=128))],
            reads=[B_scr_wdT], writes=arc(32768, 32768), dma=True)
        for ic in range(NC_I):
            so = gslot[ic]
            p.group("pe", [lambda e, so=so: e.matmul(
                big[4][:, 0:256], h128_sb, gtile_ap(so), start=True, stop=True)],
                reads=gtile_bufs(so) + [B_consts], writes=[B_big[4]])
            p.group("dve", [lambda e, so=so, pp=pp: e.tensor_tensor(
                gtile_ap(so), big[4][:, 0:256],
                wsbc_f[:, pp * 256:(pp + 1) * 256], ALU.mult)],
                reads=[B_big[4], B_imggd[0], B_imggd[1]], writes=gtile_bufs(so))
        free = [("sigA", 0), ("ysp", 0)]
        eng_i = 0
        for st in range(6):
            bit = 1 << st
            for a in range(64):
                if a & bit:
                    continue
                b = a | bit
                sa, sb_ = gslot[a], gslot[b]
                f1, f2 = free[0], free[1]
                e1 = ["dve", "dve", "dve", "dve", "dve", "gp"][eng_i % 6]; eng_i += 1
                e2 = ["dve", "dve", "dve", "dve", "dve", "gp"][eng_i % 6]; eng_i += 1
                p.group(e1, [lambda e, sa=sa, sb_=sb_, f1=f1: e.tensor_tensor(
                    gtile_ap(f1), gtile_ap(sa), gtile_ap(sb_), ALU.add)],
                    reads=gtile_bufs(sa) + gtile_bufs(sb_), writes=gtile_bufs(f1))
                p.group(e2, [lambda e, sa=sa, sb_=sb_, f2=f2: e.tensor_tensor(
                    gtile_ap(f2), gtile_ap(sa), gtile_ap(sb_), ALU.subtract)],
                    reads=gtile_bufs(sa) + gtile_bufs(sb_), writes=gtile_bufs(f2))
                gslot[a], gslot[b] = f1, f2
                free = [sa, sb_]
        for a in range(64):
            so = gslot[a]
            p.group("gpd", [lambda e, so=so, os4=os4, half=half, a=a: e.dma_start(
                scr_g[os4, a, :, half * 256:(half + 1) * 256], gtile_ap(so))],
                reads=gtile_bufs(so), writes=[B_scr_g[os4]], dma=True)

    for b in (B_consts, B_mem):
        b.frozen = True
    for b in B_scr_q + B_scr_wo + B_scr_ug + B_scr_g:
        b.frozen = True

    dbg_outs = {}

    def tap(name, ap, bufs):
        if not DEBUG:
            return
        od = nc.dram_tensor(f"dbg_{name}", list(ap.shape), ap.dtype,
                            kind="ExternalOutput").ap()
        dbg_outs[name] = od
        p.group("gpd", [lambda e, od=od, ap=ap: e.dma_start(od, ap)],
                reads=bufs, dma=True)

    for t in range(NT):
        row0 = t * TS
        p.group("gpd", [lambda e, row0=row0: e.dma_start(
            out_d[row0:row0 + TS, :], x_d[row0:row0 + TS, :])],
            writes=[B_outc[t][k] for k in range(17)], dma=True)

        for tb in range(TBN):
            o = 8 * tb
            sl, slbufs = xslot(t, tb)
            p.group("dve", [lambda e, sl=sl, o=o: e.scalar_tensor_tensor(
                xn_tm, sl, 1.0, sl, ALU.bypass, ALU.mult,
                accum_out=stat[:, o + 4:o + 5])],
                reads=slbufs, writes=[B_xn_tm, B_stat[tb]])
            p.group("dve", [lambda e, sl=sl, o=o: e.reduce_sum(
                stat[:, o:o + 1], sl, axis=AX.X)],
                reads=slbufs, writes=[B_stat[tb]])
            p.group("dve", [
                lambda e, o=o: e.tensor_scalar(stat[:, o + 2:o + 3], stat[:, o:o + 1],
                                               1.0 / D, None, ALU.mult),
                lambda e, o=o: e.tensor_scalar(stat[:, o + 3:o + 4], stat[:, o + 4:o + 5],
                                               1.0 / D, None, ALU.mult)],
                reads=[B_stat[tb]], writes=[B_stat[tb]])
            p.group("dve", [lambda e, o=o: e.tensor_tensor(
                stat[:, o + 5:o + 6], stat[:, o + 2:o + 3], stat[:, o + 2:o + 3], ALU.mult)],
                reads=[B_stat[tb]], writes=[B_stat[tb]])
            p.group("dve", [lambda e, o=o: e.tensor_tensor(
                stat[:, o + 6:o + 7], stat[:, o + 3:o + 4], stat[:, o + 5:o + 6],
                ALU.subtract)],
                reads=[B_stat[tb]], writes=[B_stat[tb]])
            p.group("act", [lambda e, o=o: e.activation(
                stat[:, o + 5:o + 6], stat[:, o + 6:o + 7], AF.Ln, bias=epsln)],
                reads=[B_stat[tb], B_consts], writes=[B_stat[tb]])
            p.group("act", [lambda e, o=o: e.activation(
                stat[:, o + 6:o + 7], stat[:, o + 5:o + 6], AF.Exp, scale=-0.5)],
                reads=[B_stat[tb]], writes=[B_stat[tb]])
            p.group("dve", [lambda e, o=o: e.scalar_tensor_tensor(
                stat[:, o + 7:o + 8], stat[:, o + 2:o + 3], -1.0, stat[:, o + 6:o + 7],
                ALU.mult, ALU.mult)],
                reads=[B_stat[tb]], writes=[B_stat[tb]])
            p.group("dve", [lambda e, sl=sl, o=o: e.tensor_scalar(
                xn_tm, sl, stat[:, o + 6:o + 7], stat[:, o + 7:o + 8],
                ALU.mult, ALU.add)],
                reads=slbufs + [B_stat[tb]], writes=[B_xn_tm])
            for c in range(NC_D):
                pb = c % 2
                sli = (c // 2) % 8
                p.group("pe", [lambda e, c=c, pb=pb, sli=sli: e.transpose(
                    ptr[pb][:, sli * 128:(sli + 1) * 128],
                    xn_tm[:, c * 128:(c + 1) * 128], idbf_sb)],
                    reads=[B_xn_tm, B_consts], writes=[B_ptr[pb]])
                dst = xnf_bf[:, c * 512 + tb * 128:c * 512 + (tb + 1) * 128]
                p.group("dve", [lambda e, dst=dst, pb=pb, sli=sli: e.tensor_copy(
                    dst, ptr[pb][:, sli * 128:(sli + 1) * 128])],
                    reads=[B_ptr[pb]],
                    writes=[B_xnf[c]] + accc(c * 1024 + tb * 256, 256))

        if t == 0:
            tap("xnf", xnf_bf, [B_xnf[c] for c in range(NC_D)] + accc(0, 16384))

        for dcg in range(8):
            slot = dcg % 2
            p.group("sync", [lambda e, dcg=dcg, slot=slot: e.dma_start(
                imgqw[slot], img_q[dcg])],
                reads=[B_scr_q[dcg]], writes=[B_imgqw[slot]], dma=True)
            if t == 0 and dcg == 0:
                tap("wqimg", imgqw[slot], [B_imgqw[slot]])
            for dd in range(2):
                dc = dcg * 2 + dd
                pb = dc % 2
                fns = [lambda e, fc=fc, dd=dd, slot=slot, pb=pb: e.matmul(
                    big[pb], imgqw[slot][:, fc * 256 + dd * 128:fc * 256 + (dd + 1) * 128],
                    xnf_bf[:, fc * 512:(fc + 1) * 512],
                    start=(fc == 0), stop=(fc == NC_D - 1)) for fc in range(NC_D)]
                p.group("pe", fns,
                        reads=[B_imgqw[slot]] + [B_xnf[fc] for fc in range(NC_D)]
                        + accc(0, 16384),
                        writes=[B_big[pb]])
                if DEBUG and t == 0 and dc == 1:
                    p.group("dve", [lambda e, pb=pb: e.tensor_copy(yspare, big[pb])],
                            reads=[B_big[pb]], writes=[B_ysp])
                    tap("q1", yspare, [B_ysp])
                p.group("act", [lambda e, pb=pb: e.activation(sigB, big[pb], AF.Exp)],
                        reads=[B_big[pb]], writes=[B_sigB])
                p.group("dve", [lambda e: e.tensor_scalar(
                    sigA, sigB, 1.0, None, ALU.min)],
                    reads=[B_sigB], writes=[B_sigA])
                p.group("dve", [lambda e, pb=pb, dc=dc: e.scalar_tensor_tensor(
                    arena_bf[:, dc * 512:(dc + 1) * 512], big[pb], 0.0, sigA,
                    ALU.max, ALU.add)],
                    reads=[B_big[pb], B_sigA], writes=arc(dc * 1024, 1024))
                if DEBUG and t == 0 and dc == 1:
                    tap("sigA1", sigA, [B_sigA])
                    tap("sigB1", sigB, [B_sigB])

        if t == 0:
            tap("sigma", arena_bf[:, 0:8192], arc(0, 16384))


        if t == 0:
            emit_g_pass(0)
            emit_g_pass(1)
            emit_ug_prep(0)

        def ytile_ap(so, bf=False):
            kind, off = so
            if kind == "fix":
                return y2_bf[:, off * 512:(off + 1) * 512]
            if kind == "acc2":
                return acc_f[:, 4096 + off * 512:4096 + (off + 1) * 512]
            if kind == "x2y":
                return x2_f[:, off * 512:(off + 1) * 512]
            t_ = (sigA if kind == "sigA" else yspare)
            tb_ = (sigA_bf if kind == "sigA" else yspare_bf)
            return tb_[:, 0:512] if bf else t_

        def ytile_bufs(so):
            kind, off = so
            if kind == "fix":
                return [B_y2[off]]
            if kind == "acc2":
                return accc(16384 + off * 2048, 2048)
            if kind == "x2y":
                return x2c(off * 2048, 2048)
            return [B_sigA if kind == "sigA" else B_ysp]

        def emit_y_phase():
            yslot = {c: (("acc2", c) if c < 8 else ("x2y", c - 8))
                     for c in range(NC_D)}
            yfree = [("sigA", 0), ("ysp", 0)]
            for c in range(NC_D):
                so = yslot[c]
                p.group("dve", [lambda e, so=so, c=c: e.tensor_scalar(
                    ytile_ap(so), xnf_bf[:, c * 512:(c + 1) * 512],
                    su_fm[:, c:c + 1], None, ALU.mult)],
                    reads=[B_xnf[c], B_consts] + accc(c * 1024, 1024),
                    writes=ytile_bufs(so))
            for st in range(4):
                bit = 1 << st
                last = (st == 3)
                for a in range(NC_D):
                    if a & bit:
                        continue
                    b = a | bit
                    sa, sb_ = yslot[a], yslot[b]
                    if last:
                        f1, f2 = ("fix", a), ("fix", b)
                    else:
                        f1, f2 = yfree[0], yfree[1]
                    p.group("dve", [lambda e, sa=sa, sb_=sb_, f1=f1, last=last: e.tensor_tensor(
                        ytile_ap(f1, bf=last), ytile_ap(sa), ytile_ap(sb_), ALU.add)],
                        reads=ytile_bufs(sa) + ytile_bufs(sb_), writes=ytile_bufs(f1))
                    p.group("dve", [lambda e, sa=sa, sb_=sb_, f2=f2, last=last: e.tensor_tensor(
                        ytile_ap(f2, bf=last), ytile_ap(sa), ytile_ap(sb_), ALU.subtract)],
                        reads=ytile_bufs(sa) + ytile_bufs(sb_), writes=ytile_bufs(f2))
                    yslot[a], yslot[b] = f1, f2
                    if not last:
                        yfree = [sa, sb_]
            return dict(yslot)

        if t == 0 or not LATE_Y:
            y1slot = emit_y_phase()


        if t == 0:
            emit_g_pass(2)
            emit_ug_prep(1)

        for hh in range(H):
            fns = []
            for tb in range(TBN):
                fns.append(lambda e, hh=hh, tb=tb: e.matmul(
                    misc[:, hh * 16 + tb * 4:hh * 16 + (tb + 1) * 4],
                    arena_bf[:, hh * 512 + tb * 128:hh * 512 + (tb + 1) * 128],
                    mn_sb.rearrange("p (n h) -> p n h", h=H)[:, :, hh],
                    start=True, stop=True))
            p.group("pe", fns, reads=arc(hh * 1024, 1024) + [B_mem], writes=[B_misc])
        if t == 0 or not LATE_Y:
            rta = x2_f[:, 0:256]
            rtb = x2_f[:, 256:512]
            rta_w, rta_r = x2c(0, 1024), x2c(0, 1024)
            rtb_w = x2c(1024, 1024)
        else:
            rta = arena_hi_f[:, 7168:7424]
            rtb = arena_hi_f[:, 7424:7680]
            rta_w, rta_r = arc(61440, 1024), arc(61440, 1024)
            rtb_w = arc(62464, 1024)
        p.group("dve", [lambda e, rta=rta: e.tensor_scalar(
            rta, misc[:, 0:256], EPS_ATT, None, ALU.max)],
            reads=[B_misc], writes=rta_w)
        p.group("dve", [lambda e, rta=rta, rtb=rtb: e.reciprocal(rtb, rta)],
                reads=rta_r, writes=rtb_w)
        p.group("dve", [lambda e, rta=rta, rtb=rtb: e.tensor_tensor(
            rta, rtb, wact4, ALU.mult)],
            reads=rtb_w + [B_consts], writes=rta_w)
        ct4 = stg[0]
        nbk = 4 if t == 0 else 6
        LAG = 2 if t == 0 else 3
        bgroups = [(hh, tb) for hh in range(H) for tb in range(TBN)]

        def b2_transpose(idx):
            hh, tb = bgroups[idx]
            slot4 = idx % 4
            pb = idx % 2
            sli = (idx // 2) % 8
            p.group("pe", [lambda e, pb=pb, sli=sli, slot4=slot4: e.transpose(
                ptr[pb][:, sli * 128:(sli + 1) * 128],
                ct4[:, slot4 * 128:(slot4 + 1) * 128], idbf_sb)],
                reads=[B_ct4[slot4], B_consts], writes=[B_ptr[pb]])
            dst_off = 16384 + hh * 1024 + tb * 256
            p.group("dve", [lambda e, pb=pb, sli=sli, dst_off=dst_off: e.tensor_copy(
                arena_bf[:, dst_off // 2:dst_off // 2 + 128],
                ptr[pb][:, sli * 128:(sli + 1) * 128])],
                reads=[B_ptr[pb]], writes=arc(dst_off, 256))

        for idx, (hh, tb) in enumerate(bgroups):
            if t == 0 and hh == 8 and tb == 0:
                emit_g_pass(3)
            bk = idx % nbk
            pap = big[bk] if bk < 5 else misc
            pbuf = B_big[bk] if bk < 5 else B_misc
            fns = []
            for n in range(NB):
                ti = hh * NB + n
                fns.append(lambda e, hh=hh, tb=tb, n=n, ti=ti, pap=pap: e.matmul(
                    pap[:, n * 128:(n + 1) * 128],
                    arena_bf[:, hh * 512 + tb * 128:hh * 512 + (tb + 1) * 128],
                    mem_sb[:, ti * 128:(ti + 1) * 128], start=True, stop=True))
            p.group("pe", fns, reads=arc(hh * 1024, 1024) + [B_mem], writes=[pbuf])
            rc = hh * 16 + tb * 4
            slot4 = idx % 4
            p.group("dve", [
                lambda e, pap=pap, rta=rta, rc=rc: e.tensor_scalar(
                    ctA, pap[:, 0:128], rta[:, rc:rc + 1], None, ALU.mult),
                lambda e, pap=pap, rta=rta, rc=rc: e.scalar_tensor_tensor(
                    ctB, pap[:, 128:256], rta[:, rc + 1:rc + 2], ctA,
                    ALU.mult, ALU.add),
                lambda e, pap=pap, rta=rta, rc=rc: e.scalar_tensor_tensor(
                    ctA, pap[:, 256:384], rta[:, rc + 2:rc + 3], ctB,
                    ALU.mult, ALU.add),
                lambda e, pap=pap, rta=rta, rc=rc, slot4=slot4: e.scalar_tensor_tensor(
                    ct4[:, slot4 * 128:(slot4 + 1) * 128], pap[:, 384:512],
                    rta[:, rc + 3:rc + 4], ctA, ALU.mult, ALU.add)],
                reads=[pbuf] + rta_r, writes=[B_ct4[slot4]])
            if idx >= LAG:
                b2_transpose(idx - LAG)
        for idx in range(len(bgroups) - LAG, len(bgroups)):
            b2_transpose(idx)

        if t > 0 and LATE_Y:
            y1slot = emit_y_phase()

        if t == 0:
            tap("comb", arena_bf[:, 8192:16384], arc(16384, 16384))

        if t == 0:
            emit_g_pass(4)
            emit_ug_prep(2)
        for os in range(4):
            if t == 0 and os == 2:
                emit_g_pass(5)
            for hf in range(2):
                slot = hf
                p.group("sync" if t == 0 else "actd",
                        [lambda e, os=os, hf=hf, slot=slot: e.dma_start(
                            imgqw[slot], img_wo[os * 2 + hf])],
                        reads=[B_scr_wo[os * 2 + hf]], writes=[B_imgqw[slot]], dma=True)
            for tb in range(TBN):
                pb = (os * TBN + tb) % 2
                fns = []
                for cf in range(NC_D):
                    hf = cf // 8
                    k = cf % 8
                    lhs_off = 16384 + cf * 1024 + tb * 256
                    fns.append(lambda e, cf=cf, hf=hf, k=k, pb=pb, lhs_off=lhs_off: e.matmul(
                        big[pb], arena_bf[:, lhs_off // 2:lhs_off // 2 + 128],
                        imgqw[hf][:, k * 512:(k + 1) * 512],
                        start=(cf == 0), stop=(cf == NC_D - 1)))
                p.group("pe", fns,
                        reads=[B_imgqw[0], B_imgqw[1]] + arc(16384, 16384),
                        writes=[B_big[pb]])
                acc_off = tb * 8192 + os * 2048
                p.group("act", [lambda e, pb=pb, tb=tb, os=os: e.activation(
                    acc_f[:, tb * 2048 + os * 512:tb * 2048 + (os + 1) * 512],
                    big[pb], AF.Copy)],
                    reads=[B_big[pb]], writes=accc(acc_off, 2048))

        if t == 0:
            tap("acc", acc_f, accc(0, 32768))
            emit_g_pass(6)
            emit_ug_prep(3)
            emit_g_pass(7)

        for c in range(NC_D):
            so = y1slot[c]
            if t == 0:
                psum = [big[4], misc][c % 2]
                psb = [B_big[4], B_misc][c % 2]
            else:
                psum = [big[2], big[3], big[4], misc][c % 4]
                psb = [B_big[2], B_big[3], B_big[4], B_misc][c % 4]
            p.group("pe", [lambda e, so=so, psum=psum: e.matmul(
                psum, h128_sb, ytile_ap(so, bf=True), start=True, stop=True)],
                reads=ytile_bufs(so) + [B_consts], writes=[psb])
            p.group("act", [lambda e, c=c, psum=psum: e.activation(
                y2_bf[:, c * 512:(c + 1) * 512], psum, AF.Copy)],
                reads=[psb], writes=[B_y2[c]])

        if t == 0:
            tap("y2", y2_bf, [B_y2[c] for c in range(NC_D)])

        gd_slots = [imggd[0], imggd[1], imgqw[1][:, 0:2048]]
        gd_bufs = [B_imggd[0], B_imggd[1], B_imgqw[1]]
        gdeng = "sync" if t == 0 else "actd"

        def emit_gd_load(li):
            os_, grp_ = li // 16, li % 16
            slot = li % 3
            sap = gd_slots[slot]
            p.group(gdeng, [lambda e, os_=os_, grp_=grp_, sap=sap: e.dma_start(
                sap.rearrange("p (g c) -> p g c", g=4),
                scr_g[os_, 4 * grp_:4 * grp_ + 4].rearrange("g p c -> p g c"))],
                reads=[B_scr_g[os_]], writes=[gd_bufs[slot]], dma=True)

        if t > 0:
            emit_gd_load(0)
            emit_gd_load(1)
        ug_slots = [imgug[0], imgug[1], imgqw[0]]
        ug_ubufs = [B_imgug[0], B_imgug[1], B_imgqw[0]]
        ug_gbufs = [B_imgug_g[0], B_imgug_g[1], B_imgqw_g]
        for ic in range(NC_I):
            slot = ic % 3
            sap = ug_slots[slot]
            p.group("sync", [lambda e, ic=ic, sap=sap: e.dma_start(
                sap[:, 0:2048], img_ug[ic, :, 0:2048])],
                reads=[B_scr_ug[ic]], writes=[ug_ubufs[slot]], dma=True)
            p.group("sync", [lambda e, ic=ic, sap=sap: e.dma_start(
                sap[:, 2048:4096], img_ug[ic, :, 2048:4096])],
                reads=[B_scr_ug[ic]], writes=[ug_gbufs[slot]], dma=True)
            if t == 0 and ic == 0:
                tap("ugimg", sap, [ug_ubufs[slot], ug_gbufs[slot]])
            ub = [big[2], big[0]][ic % 2]
            gb = [big[3], big[1]][ic % 2]
            ubf = [B_big[2], B_big[0]][ic % 2]
            gbf = [B_big[3], B_big[1]][ic % 2]
            fns = [lambda e, fc=fc, sap=sap, ub=ub: e.matmul(
                ub, sap[:, fc * 128:(fc + 1) * 128],
                y2_bf[:, fc * 512:(fc + 1) * 512],
                start=(fc == 0), stop=(fc == NC_D - 1)) for fc in range(NC_D)]
            p.group("pe", fns,
                    reads=[ug_ubufs[slot]] + [B_y2[c] for c in range(NC_D)],
                    writes=[ubf])
            fns = [lambda e, fc=fc, sap=sap, gb=gb: e.matmul(
                gb, sap[:, 2048 + fc * 128:2048 + (fc + 1) * 128],
                y2_bf[:, fc * 512:(fc + 1) * 512],
                start=(fc == 0), stop=(fc == NC_D - 1)) for fc in range(NC_D)]
            p.group("pe", fns,
                    reads=[ug_gbufs[slot]] + [B_y2[c] for c in range(NC_D)],
                    writes=[gbf])
            p.group("act", [lambda e, ub=ub, ic=ic: e.activation(
                usb, ub, AF.Copy, scale=sclu_fm[:, ic:ic + 1])],
                reads=[ubf, B_consts], writes=[B_usb])
            p.group("act", [lambda e, gb=gb, ic=ic: e.activation(
                gsb, gb, AF.Silu, scale=sclg_fm[:, ic:ic + 1])],
                reads=[gbf, B_consts], writes=[B_gsb])
            p.group("dve", [lambda e, ic=ic: e.tensor_tensor(
                arena_bf[:, ic * 512:(ic + 1) * 512], usb, gsb, ALU.mult)],
                reads=[B_usb, B_gsb], writes=arc(ic * 1024, 1024))

        if t == 0:
            tap("zp", arena_bf, sum((arc(ic * 1024, 1024) for ic in range(NC_I)), []))

        if t + 1 < NT:
            for tb in (0, 1):
                sl, slbufs = xslot(t + 1, tb)
                p.group("sync", [lambda e, t=t, tb=tb, sl=sl: e.dma_start(
                    sl, x_d[(t + 1) * TS + tb * 128:(t + 1) * TS + (tb + 1) * 128, :])],
                    writes=slbufs, dma=True)
        for os in range(4):
            bk = [tb for tb in range(TBN)]
            for grp in range(16):
                li = os * 16 + grp
                if t == 0:
                    emit_gd_load(li)
                elif li + 2 < 64:
                    emit_gd_load(li + 2)
                slot = li % 3
                sap = gd_slots[slot]
                if t == 0 and os == 0 and grp == 0:
                    tap("gdimg", sap, [gd_bufs[slot]])
                for g in range(4):
                    ic = grp * 4 + g
                    fns = [lambda e, ic=ic, g=g, sap=sap, tb=tb, bk=bk: e.matmul(
                        big[bk[tb]], arena_bf[:, ic * 512 + tb * 128:ic * 512 + (tb + 1) * 128],
                        sap[:, g * 512:(g + 1) * 512],
                        start=(ic == 0), stop=(ic == NC_I - 1)) for tb in range(TBN)]
                    p.group("pe", fns,
                            reads=[gd_bufs[slot]] + arc(ic * 1024, 1024),
                            writes=[B_big[bk[tb]] for tb in range(TBN)])
            for tb in range(TBN):
                sslot = os % 2
                soff = sslot * 8192 + tb * 2048
                p.group("dve", [lambda e, tb=tb, os=os, sslot=sslot, bk=bk: e.scalar_tensor_tensor(
                    x2_f[:, sslot * 2048 + tb * 512:sslot * 2048 + (tb + 1) * 512],
                    big[bk[tb]], 1.0,
                    acc_f[:, tb * 2048 + os * 512:tb * 2048 + (os + 1) * 512],
                    ALU.bypass, ALU.add)],
                    reads=[B_big[bk[tb]]] + accc(tb * 8192 + os * 2048, 2048),
                    writes=x2c(soff, 2048))
                p.group("gpd", [lambda e, tb=tb, os=os, sslot=sslot, row0=row0: e.dma_start(
                    out_d[row0 + tb * 128:row0 + (tb + 1) * 128, os * 512:(os + 1) * 512],
                    x2_f[:, sslot * 2048 + tb * 512:sslot * 2048 + (tb + 1) * 512],
                    accum_op=ALU.add)],
                    reads=x2c(soff, 2048), writes=[B_outc[t][tb * 4 + os]], dma=True)
        if t + 1 < NT:
            for tb in (2, 3):
                sl, slbufs = xslot(t + 1, tb)
                p.group("sync", [lambda e, t=t, tb=tb, sl=sl: e.dma_start(
                    sl, x_d[(t + 1) * TS + tb * 128:(t + 1) * TS + (tb + 1) * 128, :])],
                    writes=slbufs, dma=True)

    semname = {k: k for k in STREAM}
    with (
        nc.semaphore("s_pe") as s_pe,
        nc.semaphore("s_dve") as s_dve,
        nc.semaphore("s_act") as s_act,
        nc.semaphore("s_actd") as s_actd,
        nc.semaphore("s_gp") as s_gp,
        nc.semaphore("s_gpd") as s_gpd,
        nc.semaphore("s_sync") as s_sync,
        nc.Block() as block,
    ):
        sems = {"pe": s_pe, "dve": s_dve, "act": s_act, "actd": s_actd,
                "gp": s_gp, "gpd": s_gpd, "sync": s_sync}

        def replay(eng, stream):
            for waits, fns, key, dma in p.ops[stream]:
                for sk, v in waits:
                    eng.wait_ge(sems[sk], v)
                if dma:
                    for fn in fns:
                        fn(eng).then_inc(sems[key], 16)
                else:
                    insts = [fn(eng) for fn in fns]
                    insts[-1].then_inc(sems[key], 1)

        @block.tensor
        def _(e):
            replay(e, "pe")

        @block.vector
        def _(e):
            replay(e, "dve")

        @block.scalar
        def _(e):
            replay(e, "act")

        @block.gpsimd
        def _(e):
            replay(e, "gp")
            e.wait_ge(sems["gpd"], p.cnt["gpd"])

        @block.sync
        def _(e):
            replay(e, "sync")
            e.wait_ge(sems["gpd"], p.cnt["gpd"])
            e.wait_ge(sems["actd"], p.cnt["actd"])

    return nc


_cache = {}


def kernel(**inputs):
    if "nc" not in _cache:
        _cache["nc"] = build_program()
    nc = _cache["nc"]

    h128 = hadamard(128).astype(ml_dtypes.bfloat16)
    idbf = np.eye(128).astype(ml_dtypes.bfloat16)
    idf = np.eye(128, dtype=np.float32)
    nsel = np.zeros((64, NB), dtype=np.float32)
    for n in range(NB):
        for hh in range(H):
            nsel[n * H + hh, n] = 1.0

    x = np.ascontiguousarray(inputs["L0_attn_hidden_states"])
    common = {
        "wq": np.ascontiguousarray(inputs["L0_attn_Wq"]),
        "wo": np.ascontiguousarray(inputs["L0_attn_Wo"]),
        "bw": np.ascontiguousarray(inputs["L0_attn_bank_weights"]),
        "mem": np.ascontiguousarray(inputs["L0_attn_memories"]),
        "mn": np.ascontiguousarray(inputs["L0_attn_memory_norms"]),
        "suug": np.ascontiguousarray(inputs["L0_ffn_SU_ug"]),
        "sudp": np.ascontiguousarray(inputs["L0_ffn_SU_dp"]),
        "wsug": np.ascontiguousarray(inputs["L0_ffn_Wscale_ug"]),
        "wsdp": np.ascontiguousarray(inputs["L0_ffn_Wscale_dp"]),
        "wup": np.ascontiguousarray(inputs["L0_ffn_W_up"]),
        "wgate": np.ascontiguousarray(inputs["L0_ffn_W_gate"]),
        "wdown": np.ascontiguousarray(inputs["L0_ffn_W_down"]),
        "h128": h128, "idbf": idbf, "idf": idf, "nsel": nsel,
    }
    in_maps = [dict(common, x=np.ascontiguousarray(x[c])) for c in range(NCORES)]
    res = run_bass_kernel_spmd(nc, in_maps, core_ids=list(range(NCORES)))
    out = np.stack([res.results[c]["out"] for c in range(NCORES)], axis=0)
    return out.astype(np.float32)
